# version 06
# baseline (speedup 1.0000x reference)
import numpy as np
import concourse.bass as bass
import concourse.mybir as mybir
from concourse.bass_utils import run_bass_kernel_spmd

F32 = mybir.dt.float32
BF16 = mybir.dt.bfloat16
U8 = mybir.dt.uint8
AF = mybir.ActivationFunctionType
ALU = mybir.AluOpType
AX = mybir.AxisListType

D = 1024
T = 2048
TH = 1024
NE = 32
NCORE = 8

X_OFF = 0
XN_OFF = 32768
X2N_OFF = 49152
HG_OFF = 81920
ATTN_OFF = 114688
SLABA_OFF = 131072
WOUT_OFF = 147456
BC_OFF = 180224
TMP_OFF = 188416
MISC_OFF = 201728
ARENA_BYTES = 209408
APRIME_OFF = 32768
SILU_OFF = 196608
FIN_OFF = 81920
FSLOT_OFF = 147456
R2_OFF = 0
ZBUF_OFF = BC_OFF
HJUNK_OFF = TMP_OFF
TMP_STAGE = TMP_OFF
TMP_A = TMP_OFF + 2304
TMP_B = TMP_OFF + 4352
TMP_C = TMP_OFF + 6400
TMP_D = TMP_OFF + 8448
TMP_E = TMP_OFF + 10496
MS_SQ0 = MISC_OFF
MS_SQ1 = MISC_OFF + 1024
MS_WTM = MISC_OFF + 2048
MS_KLAST = MISC_OFF + 4096
MS_CARRY = MISC_OFF + 4160
MS_SCOL = MISC_OFF + 4224
MS_RMS = MISC_OFF + 4256
MS_SPL = MISC_OFF + 4288
MS_BIASB = MISC_OFF + 4320
MS_IDENTBF = MISC_OFF + 4448
MS_OSCBF = MISC_OFF + 4704
MS_HSUM = MISC_OFF + 4960
MS_HSQ = MISC_OFF + 5024
MS_HRSTD = MISC_OFF + 5088
MS_HNEGMU = MISC_OFF + 5152
MS_OSCF = MISC_OFF + 5248
MS_IDENTF = MISC_OFF + 5760
MS_GWT = MISC_OFF + 6272
MS_ONES1 = MISC_OFF + 6784
MS_EPS = MISC_OFF + 7424
MS_GBIAS = MISC_OFF + 7296
E_SCF = SLABA_OFF
E_SCTM = SLABA_OFF + 4096
E_SC = SLABA_OFF + 5120
E_EQ = SLABA_OFF + 6144
E_SEL = SLABA_OFF + 7168
E_X8 = SLABA_OFF + 8192
E_WRAW = SLABA_OFF + 9216
E_M1 = SLABA_OFF + 10240
E_M2 = SLABA_OFF + 10368
E_GS = SLABA_OFF + 10496
E_G1 = SLABA_OFF + 10624
E_G2 = SLABA_OFF + 10656
E_GM = SLABA_OFF + 10688
E_MI = SLABA_OFF + 10816
E_GI = SLABA_OFF + 10848
E_DEN = SLABA_OFF + 10880
E_REC = SLABA_OFF + 10912
BC2_MU = SLABA_OFF + 11264
BC2_RSTD = SLABA_OFF + 13312

EB = {F32: 4, BF16: 2}

PH_ORDER = ["A", "B0", "C10", "C20", "D0", "E0",
            "B1", "C11", "C21", "D1", "E1", "F", "H"]


class Sem:
    def __init__(self, nc, name):
        self.h = nc.alloc_semaphore(name=name)
        self.v = 0


def _replay(eng, q):
    for it in q:
        k = it[0]
        if k == "w":
            eng.wait_ge(it[1], it[2])
        elif k == "o":
            getattr(eng, it[1])(*it[2], **it[3])
        elif k == "d":
            eng.dma_start(it[1], it[2]).then_inc(it[3], 16)
        else:
            eng.drain().then_inc(it[1], it[2])


class Prog:
    def __init__(self):
        self.q = {k: [] for k in ("t", "v", "s", "g", "y")}
        self.hwm = {k: {} for k in ("t", "v", "s", "g", "y")}

    def wait(self, e, sem, val=None):
        v = sem.v if val is None else val
        if v > 0 and v > self.hwm[e].get(id(sem), -1):
            self.hwm[e][id(sem)] = v
            self.q[e].append(("w", sem.h, v))

    def op(self, e, meth, *args, **kw):
        self.q[e].append(("o", meth, args, kw))

    def dma(self, e, sem, out, in_):
        sem.v += 16
        self.q[e].append(("d", out, in_, sem.h))
        return sem.v

    def di(self, e, sem, n=1):
        sem.v += n
        self.q[e].append(("i", sem.h, n))
        return sem.v

    def replay_into(self, nc):
        with nc.Block() as blk:
            if self.q["t"]:
                @blk.tensor
                def _(eng, q=self.q["t"]):
                    _replay(eng, q)
            if self.q["v"]:
                @blk.vector
                def _(eng, q=self.q["v"]):
                    _replay(eng, q)
            if self.q["s"]:
                @blk.scalar
                def _(eng, q=self.q["s"]):
                    _replay(eng, q)
            if self.q["g"]:
                @blk.gpsimd
                def _(eng, q=self.q["g"]):
                    _replay(eng, q)
            if self.q["y"]:
                @blk.sync
                def _(eng, q=self.q["y"]):
                    _replay(eng, q)


def build_program(debug_taps=(), upto="H"):
    upto_i = PH_ORDER.index(upto)
    nc = bass.Bass("TRN2", target_bir_lowering=False)

    xin = nc.dram_tensor("xin", [D, T], F32, kind="ExternalInput")
    wk = nc.dram_tensor("wk", [D, 2048], F32, kind="ExternalInput")
    wugg = nc.dram_tensor("wugg", [D, 6144], F32, kind="ExternalInput")
    wout = nc.dram_tensor("wout", [2048, D], F32, kind="ExternalInput")
    rms_col = nc.dram_tensor("rms_col", [128, 8], F32, kind="ExternalInput")
    split_col = nc.dram_tensor("split_col", [128, 8], F32, kind="ExternalInput")
    gwt = nc.dram_tensor("gwt", [D, NE], F32, kind="ExternalInput")
    gbias = nc.dram_tensor("gbias", [1, NE], F32, kind="ExternalInput")
    eg = nc.dram_tensor("eg", [NE, D, 512], F32, kind="ExternalInput")
    eu = nc.dram_tensor("eu", [NE, D, 512], F32, kind="ExternalInput")
    ed = nc.dram_tensor("ed", [NE, 512, D], F32, kind="ExternalInput")
    sg = nc.dram_tensor("sg", [D, D], F32, kind="ExternalInput")
    su = nc.dram_tensor("su", [D, D], F32, kind="ExternalInput")
    sd = nc.dram_tensor("sd", [D, D], F32, kind="ExternalInput")
    osc = nc.dram_tensor("osc", [128, 128], F32, kind="ExternalInput")
    ident = nc.dram_tensor("ident", [128, 128], F32, kind="ExternalInput")
    ones1 = nc.dram_tensor("ones1", [1, 128], F32, kind="ExternalInput")
    ydram = nc.dram_tensor("y", [D, T], F32, kind="ExternalOutput")
    r2d = nc.dram_tensor("r2d", [D, T], F32,
                         kind="ExternalOutput" if "r2" in debug_taps else "Internal")
    tapd = {}
    for name, shape, dt in (("xn", [D, TH], BF16), ("hg", [2048, TH], BF16),
                            ("attn", [D, TH], BF16), ("x2n", [D, T], BF16),
                            ("wtm", [128, 512], F32), ("fin", [128, 16384], F32),
                            ("scores", [32, TH], F32), ("rdbg", [128, 1104], F32),
                            ("hdbg", [128, 64], F32)):
        if name in debug_taps:
            tapd[name] = nc.dram_tensor("tap_" + name, shape, dt, kind="ExternalOutput")

    arena_cm = nc.sbuf_tensor("arena", [128, ARENA_BYTES], U8)
    ps_cm = nc.psum_tensor("ps", [128, 4096], F32)
    arena = arena_cm.__enter__()
    psA = ps_cm.__enter__()

    def S(off, cols, dt=F32, parts=128, p0=0):
        return arena[p0:p0 + parts, off:off + cols * EB[dt]].bitcast(dt)

    def P(bank, off=0, cols=512, parts=128):
        return psA[0:parts, bank * 512 + off: bank * 512 + off + cols]

    sLD = Sem(nc, "sLD")
    sWL = Sem(nc, "sWL")
    sSP = Sem(nc, "sSP")
    sPE = Sem(nc, "sPE")
    sSC = Sem(nc, "sSC")
    sVE = Sem(nc, "sVE")

    progs = []

    def phase():
        p = Prog()
        progs.append(p)
        return p

    def Xc(c, tc, n=512):
        return S(X_OFF + c * 4096, 1024)[:, tc:tc + n]

    def XNc(kc, tc, n=512):
        return S(XN_OFF + kc * 2048, 1024, BF16)[:, tc:tc + n]

    def X2Nc(kc, col, n=512):
        return S(X2N_OFF + kc * 4096, 2048, BF16)[:, col:col + n]

    def HGc(hc, tc, n=512):
        return S(HG_OFF + hc * 2048, 1024, BF16)[:, tc:tc + n]

    def ATTNc(c, tc, n=512):
        return S(ATTN_OFF + c * 2048, 1024, BF16)[:, tc:tc + n]

    def FINt(tg):
        return S(FIN_OFF + tg * 4096, 1024)

    ones1_sb = S(MS_ONES1, 128, F32, parts=1)
    gbias_sb = S(MS_GBIAS, 32, F32, parts=1)
    xload_val = [0]
    r2_reload_val = [0]
    wl_slot = {}

    def emit_c1_slot_load(p, hc, wait_pe_val=None):
        slot_off = SLABA_OFF + (hc % 2) * 8192
        if wait_pe_val:
            p.wait("g", sPE, wait_pe_val)
        v = 0
        for wi, (src, col0) in enumerate(
                ((wk, hc * 128), (wugg, hc * 128),
                 (wugg, 2048 + hc * 128), (wugg, 4096 + hc * 128))):
            v = p.dma("g", sWL,
                      S(slot_off + wi * 2048, 1024, BF16).rearrange(
                          "p (kc j) -> p kc j", kc=8),
                      src[:, col0:col0 + 128].rearrange(
                          "(kc p) j -> p kc j", kc=8, p=128))
        wl_slot[hc] = v

    pA = phase()
    va_x = pA.dma("y", sLD, S(X_OFF, 8192).rearrange("p (c t) -> p c t", c=8),
                  xin[:, :].rearrange("(c p) t -> p c t", c=8, p=128)[:, :, 0:TH])
    pA.dma("y", sLD, S(MS_RMS, 8), rms_col[:, :])
    pA.dma("y", sLD, S(MS_SPL, 8), split_col[:, :])
    va_oscf = pA.dma("y", sLD, S(MS_OSCF, 128), osc[:, :])
    va_identf = pA.dma("y", sLD, S(MS_IDENTF, 128), ident[:, :])
    pA.dma("y", sLD, ones1_sb, ones1[:, :])
    va_small = pA.dma("y", sLD, gbias_sb, gbias[:, :])
    vg_identbf = pA.dma("g", sWL, S(MS_IDENTBF, 128, BF16), ident[:, :])
    vg_oscbf = pA.dma("g", sWL, S(MS_OSCBF, 128, BF16), osc[:, :])
    vg_gwt = pA.dma("g", sWL,
                    S(MS_GWT, 256, BF16).rearrange("p (kc e) -> p kc e", kc=8),
                    gwt[:, :].rearrange("(kc p) e -> p kc e", kc=8, p=128))
    vg_wout = pA.dma("g", sWL,
                     S(WOUT_OFF, 16384, BF16).rearrange("p (kc o) -> p kc o", kc=16),
                     wout[:, :].rearrange("(kc p) o -> p kc o", kc=16, p=128))
    emit_c1_slot_load(pA, 0)
    emit_c1_slot_load(pA, 1)
    pA.op("v", "memset", S(MS_KLAST, 16), 0.0)
    pA.op("v", "memset", S(MS_EPS, 1), 1e-5)
    pA.op("v", "memset", S(MS_CARRY, 16), 0.0)
    pA.wait("v", sLD, va_small)
    pA.op("v", "tensor_mul", S(MS_SCOL, 8), S(MS_RMS, 8), S(MS_SPL, 8))
    pA.di("v", sVE)

    def phase_B(h):
        p = phase()
        xv = va_x if h == 0 else xload_val[0]
        p.wait("t", sLD, xv)
        p.wait("t", sLD, va_oscf)
        p.wait("t", sWL, vg_oscbf)
        p.wait("s", sLD, xv)
        scol = S(MS_SCOL, 8)
        vpe_prev = 0
        ve_end_prev = 0
        for tt in range(2):
            tc = tt * 512
            mu_ps = P(4 * tt + 0)
            sq_ps = P(4 * tt + 1)
            sqv = []
            for c in range(8):
                if c == 0 and vpe_prev:
                    p.wait("s", sPE, vpe_prev)
                    p.wait("s", sVE, ve_end_prev)
                p.op("s", "activation", S(HG_OFF + c * 1024, 512, BF16),
                     Xc(c, tc), AF.Square)
                sqv.append(p.di("s", sSC))
            for c in range(8):
                p.op("t", "matmul", mu_ps, S(MS_OSCF, 128), Xc(c, tc),
                     start=(c == 0), stop=(c == 7))
            for c in range(8):
                p.wait("t", sSC, sqv[c])
                p.op("t", "matmul", sq_ps, S(MS_OSCBF, 128, BF16),
                     S(HG_OFF + c * 1024, 512, BF16),
                     start=(c == 0), stop=(c == 7))
            vpe = p.di("t", sPE)
            vpe_prev = vpe
            t0, t1, t2, t3, t4, t5 = (S(TMP_A, 512), S(TMP_B, 512), S(TMP_C, 512),
                                      S(TMP_D, 512), S(TMP_E, 512),
                                      S(TMP_STAGE, 512))
            p.wait("s", sPE, vpe)
            p.op("s", "activation", t0, mu_ps, AF.Square)
            p.op("s", "copy", S(BC_OFF, 1024)[:, tc:tc + 512], mu_ps)
            vsq = p.di("s", sSC)
            p.wait("v", sPE, vpe)
            p.wait("v", sSC, vsq)
            p.op("v", "tensor_sub", t1, sq_ps, t0)
            vve = p.di("v", sVE)
            p.wait("s", sVE, vve)
            p.op("s", "activation", t2, t1, AF.Sqrt, bias=S(MS_EPS, 1))
            p.op("s", "activation", t3, t1, AF.Sqrt)
            vsc = p.di("s", sSC)
            p.wait("v", sSC, vsc)
            p.op("v", "reciprocal", S(BC_OFF + 4096, 1024)[:, tc:tc + 512], t2)
            p.op("v", "scalar_tensor_tensor", t0, t2, 1e-8, t3, ALU.mult, ALU.add)
            p.op("v", "reciprocal", t4, t0)
            for c in range(8):
                p.op("v", "tensor_sub", t5, Xc(c, tc), mu_ps)
                p.op("v", "scalar_tensor_tensor", XNc(c, tc), t5,
                     scol[:, c:c + 1], t4, ALU.mult, ALU.mult)
            ve_end_prev = p.di("v", sVE)
        if "xn" in tapd and h == 0:
            p.wait("y", sVE)
            for c in range(8):
                p.dma("y", sSP, tapd["xn"][c * 128:(c + 1) * 128, :],
                      S(XN_OFF + c * 2048, 1024, BF16))
            p.wait("y", sSP)
        return p

    def phase_C1(h):
        p = phase()
        if h == 1:
            emit_c1_slot_load(p, 0)
            emit_c1_slot_load(p, 1)
        sc_done = {}
        ve_done = {}
        pe_done = {}
        for hc in range(16):
            slot_off = SLABA_OFF + (hc % 2) * 8192
            for tt in range(2):
                k = hc * 2 + tt
                par = k % 2
                pk, pgi, pu, pgo = (P(4 * par + 0), P(4 * par + 1),
                                    P(4 * par + 2), P(4 * par + 3))
                p.wait("t", sWL, wl_slot[hc])
                if k >= 2:
                    p.wait("t", sSC, sc_done[k - 2])
                    p.wait("t", sVE, ve_done[k - 2])
                for wi, dst in ((0, pk), (2, pgi), (1, pu), (3, pgo)):
                    for kc in range(8):
                        p.op("t", "matmul", dst,
                             S(slot_off + wi * 2048 + kc * 256, 128, BF16),
                             XNc(kc, tt * 512), start=(kc == 0), stop=(kc == 7))
                pe_done[k] = p.di("t", sPE)
                stage = S(TMP_STAGE, 514)
                tmKm, tmGi, tmGo, tmU, tmH = (S(TMP_A, 512), S(TMP_B, 512),
                                              S(TMP_C, 512), S(TMP_D, 512),
                                              S(TMP_E, 512))
                p.wait("s", sPE, pe_done[k])
                if k >= 1:
                    p.wait("s", sVE, ve_done[k - 1])
                p.op("s", "copy", stage[:, 0:1], S(MS_KLAST, 16)[:, hc:hc + 1])
                p.op("s", "activation", stage[:, 1:513], pk, AF.Sigmoid)
                p.op("s", "drain")
                p.op("s", "copy", S(MS_KLAST, 16)[:, hc:hc + 1], stage[:, 512:513])
                p.op("s", "activation", tmKm, pk, AF.Sigmoid, scale=-1.0)
                p.op("s", "activation", tmGi, pgi, AF.Sigmoid)
                p.op("s", "activation", tmGo, pgo, AF.Sigmoid)
                sc_done[k] = p.di("s", sSC)
                p.wait("v", sSC, sc_done[k])
                p.op("v", "tensor_mul", tmGi, tmGi, tmKm)
                p.op("v", "tensor_mul", tmU, pu, tmGi)
                p.op("v", "tensor_tensor_scan", tmH, stage[:, 0:512], tmU,
                     S(MS_CARRY, 16)[:, hc:hc + 1], ALU.mult, ALU.add)
                p.op("v", "drain")
                p.op("v", "tensor_copy", S(MS_CARRY, 16)[:, hc:hc + 1],
                     tmH[:, 511:512])
                p.op("v", "tensor_mul", HGc(hc, tt * 512), tmH, tmGo)
                ve_done[k] = p.di("v", sVE)
            if hc + 2 < 16:
                emit_c1_slot_load(p, hc + 2, wait_pe_val=pe_done[hc * 2 + 1])
        if "hg" in tapd and h == 0:
            p.wait("y", sVE)
            for hc in range(16):
                p.dma("y", sSP, tapd["hg"][hc * 128:(hc + 1) * 128, :],
                      S(HG_OFF + hc * 2048, 1024, BF16))
            p.wait("y", sSP)
        return p

    def phase_C2(h):
        p = phase()
        p.wait("t", sWL, vg_wout)
        ve2 = {}
        for tt in range(2):
            tc = tt * 512
            for seth in range(2):
                m = tt * 2 + seth
                par = m % 2
                if m >= 2:
                    p.wait("t", sVE, ve2[m - 2])
                for oc in range(4):
                    c = seth * 4 + oc
                    dst = P(4 * par + oc)
                    for kc in range(16):
                        p.op("t", "matmul", dst,
                             S(WOUT_OFF + kc * 2048 + c * 256, 128, BF16),
                             HGc(kc, tc), start=(kc == 0), stop=(kc == 15))
                vpe = p.di("t", sPE)
                p.wait("v", sPE, vpe)
                for oc in range(4):
                    c = seth * 4 + oc
                    tmp0 = S(TMP_A, 512)
                    p.op("v", "tensor_sub", tmp0, Xc(c, tc),
                         S(BC_OFF, 1024)[:, tc:tc + 512])
                    p.op("v", "tensor_mul", tmp0, tmp0,
                         S(BC_OFF + 4096, 1024)[:, tc:tc + 512])
                    p.op("v", "tensor_add", ATTNc(c, tc), tmp0, P(4 * par + oc))
                ve2[m] = p.di("v", sVE)
        if "attn" in tapd and h == 0:
            p.wait("y", sVE)
            for c in range(8):
                p.dma("y", sSP, tapd["attn"][c * 128:(c + 1) * 128, :],
                      S(ATTN_OFF + c * 2048, 1024, BF16))
            p.wait("y", sSP)
        return p

    def phase_D(h):
        p = phase()
        p.wait("t", sLD, va_oscf)
        vpe3_prev = 0
        ve_end_prev = 0
        for tt in range(2):
            tc = tt * 512
            mu_ps = P(4 * tt + 0)
            sq_ps = P(4 * tt + 1)
            sqv = []
            for c in range(8):
                if c == 0 and vpe3_prev:
                    p.wait("s", sPE, vpe3_prev)
                    p.wait("s", sVE, ve_end_prev)
                p.op("s", "activation", S(HG_OFF + c * 1024, 512, BF16),
                     ATTNc(c, tc), AF.Square)
                sqv.append(p.di("s", sSC))
            for c in range(8):
                p.op("t", "matmul", mu_ps, S(MS_OSCBF, 128, BF16), ATTNc(c, tc),
                     start=(c == 0), stop=(c == 7))
            for c in range(8):
                p.wait("t", sSC, sqv[c])
                p.op("t", "matmul", sq_ps, S(MS_OSCBF, 128, BF16),
                     S(HG_OFF + c * 1024, 512, BF16),
                     start=(c == 0), stop=(c == 7))
            vpe = p.di("t", sPE)
            t0, t1, t2 = S(TMP_A, 512), S(TMP_B, 512), S(TMP_C, 512)
            p.wait("s", sPE, vpe)
            p.op("s", "activation", t0, mu_ps, AF.Square)
            p.op("s", "copy", S(BC2_MU, 512), mu_ps)
            vsq = p.di("s", sSC)
            p.wait("v", sPE, vpe)
            p.wait("v", sSC, vsq)
            p.op("v", "tensor_sub", t1, sq_ps, t0)
            vve = p.di("v", sVE)
            p.wait("s", sVE, vve)
            p.op("s", "activation", t2, t1, AF.Sqrt, bias=S(MS_EPS, 1))
            vsc = p.di("s", sSC)
            p.wait("v", sSC, vsc)
            p.op("v", "reciprocal", S(BC2_RSTD, 512), t2)
            for c in range(8):
                tmp0 = S(TMP_D, 512)
                p.op("v", "tensor_sub", tmp0, ATTNc(c, tc), S(BC2_MU, 512))
                p.op("v", "tensor_mul", tmp0, tmp0, S(BC2_RSTD, 512))
                p.op("v", "tensor_add", Xc(c, tc), tmp0, Xc(c, tc))
            vd2 = p.di("v", sVE)
            mu3_ps = P(4 * tt + 2)
            sq3_ps = P(4 * tt + 3)
            sq3 = []
            p.wait("s", sVE, vd2)
            for c in range(8):
                p.op("s", "activation", S(HG_OFF + 8192 + c * 1024, 512, BF16),
                     Xc(c, tc), AF.Square)
                sq3.append(p.di("s", sSC))
            p.wait("t", sVE, vd2)
            for c in range(8):
                p.op("t", "matmul", mu3_ps, S(MS_OSCF, 128), Xc(c, tc),
                     start=(c == 0), stop=(c == 7))
            for c in range(8):
                p.wait("t", sSC, sq3[c])
                p.op("t", "matmul", sq3_ps, S(MS_OSCBF, 128, BF16),
                     S(HG_OFF + 8192 + c * 1024, 512, BF16),
                     start=(c == 0), stop=(c == 7))
            vpe3 = p.di("t", sPE)
            vpe3_prev = vpe3
            p.wait("s", sPE, vpe3)
            p.op("s", "activation", t0, mu3_ps, AF.Square)
            vsq3 = p.di("s", sSC)
            p.wait("v", sPE, vpe3)
            p.wait("v", sSC, vsq3)
            p.op("v", "tensor_sub", t1, sq3_ps, t0)
            vve3 = p.di("v", sVE)
            p.wait("s", sVE, vve3)
            p.op("s", "activation", t2, t1, AF.Sqrt, bias=S(MS_EPS, 1))
            vsc3 = p.di("s", sSC)
            p.wait("v", sSC, vsc3)
            p.op("v", "reciprocal", t0, t2)
            for c in range(8):
                tmp0 = S(TMP_D, 512)
                p.op("v", "tensor_sub", tmp0, Xc(c, tc), mu3_ps)
                p.op("v", "tensor_mul", X2Nc(c, h * 1024 + tc), tmp0, t0)
            ve_end_prev = p.di("v", sVE)
        p.wait("y", sVE)
        spill_v = p.dma(
            "y", sSP,
            r2d[:, :].rearrange("(c p) t -> p c t", c=8, p=128)[:, :, h * TH:(h + 1) * TH],
            S(X_OFF, 8192).rearrange("p (c t) -> p c t", c=8))
        return p, spill_v

    def phase_E(h, spill_v):
        p = phase()
        scf = S(E_SCF, 1024, F32, parts=32)
        p.wait("t", sWL, vg_gwt)
        vs_prev = 0
        for tt in range(2):
            scps = P(0, 0, 512, parts=32)
            if vs_prev:
                p.wait("t", sSC, vs_prev)
            for kc in range(8):
                p.op("t", "matmul", scps, S(MS_GWT + kc * 64, 32, BF16),
                     X2Nc(kc, h * 1024 + tt * 512), start=(kc == 0), stop=(kc == 7))
            vpe = p.di("t", sPE)
            p.wait("s", sPE, vpe)
            p.op("s", "activation", scf[:, tt * 512:(tt + 1) * 512], scps, AF.Sigmoid)
            vs_prev = p.di("s", sSC)
        if "scores" in tapd and h == 0:
            p.wait("y", sSC)
            p.dma("y", sSP, tapd["scores"][:, :], scf)
            p.wait("y", sSP)
        p.wait("t", sLD, va_identf)
        sctm = S(E_SCTM, 256)
        vtr = {}
        for tk in range(8):
            trps = P(2 + tk % 2, 0, 32)
            if tk >= 2:
                p.wait("t", sSC, vtr[tk - 2])
            p.op("t", "transpose", trps, scf[:, tk * 128:(tk + 1) * 128],
                 S(MS_IDENTF, 128)[0:32, 0:32])
            vpe = p.di("t", sPE)
            p.wait("s", sPE, vpe)
            p.op("s", "copy", sctm[:, tk * 32:(tk + 1) * 32], trps)
            vtr[tk] = p.di("s", sSC)
        if h == 0:
            p.wait("t", sLD, va_small)
            bb = P(4, 0, 32)
            p.op("t", "matmul", bb, ones1_sb, gbias_sb, start=True, stop=True)
            vpe = p.di("t", sPE)
            p.wait("s", sPE, vpe)
            p.op("s", "copy", S(MS_BIASB, 32), bb)
            p.di("s", sSC)
        p.wait("v", sSC)
        sc, eqb, sel, x8, wraw = (S(E_SC, 256), S(E_EQ, 256), S(E_SEL, 256),
                                  S(E_X8, 256), S(E_WRAW, 256))
        m1, m2, gs = S(E_M1, 32), S(E_M2, 32), S(E_GS, 32)
        g1, g2 = S(E_G1, 8), S(E_G2, 8)
        gm, mi, gi = S(E_GM, 32), S(E_MI, 8), S(E_GI, 8)
        den, rec = S(E_DEN, 8), S(E_REC, 8)

        def v4(ap):
            return ap.rearrange("p (t g e) -> p t g e", t=8, g=4, e=8)

        def v3g(ap):
            return ap.rearrange("p (t g) -> p t g", t=8, g=4)

        def v3e(ap):
            return ap.rearrange("p (t e) -> p t e", t=8, e=32)

        def col8(ap):
            return ap.rearrange("p (t o) -> p t o", t=8)

        bias_view = S(MS_BIASB, 32).rearrange("p (o e) -> p o e", o=1) \
            .broadcast_to([128, 8, 32])

        def vop(meth, *args, **kw):
            p.op("v", meth, *args, **kw)
            p.op("v", "drain")

        vop("tensor_tensor", v3e(sc), v3e(sctm), bias_view, ALU.add)
        vop("reduce_max", v3g(m1), v4(sc), AX.X)
        m1b = v3g(m1).unsqueeze(3).broadcast_to([128, 8, 4, 8])
        vop("tensor_tensor", v4(eqb), v4(sc), m1b, ALU.is_equal)
        vop("scalar_tensor_tensor", x8, eqb, -1e30, sc, ALU.mult, ALU.add)
        vop("reduce_max", v3g(m2), v4(x8), AX.X)
        vop("tensor_add", gs, m1, m2)
        vop("reduce_max", col8(g1), v3g(gs), AX.X)
        g1b = col8(g1).broadcast_to([128, 8, 4])
        vop("tensor_tensor", v3g(m1), v3g(gs), g1b, ALU.is_equal)
        vop("scalar_tensor_tensor", m2, m1, -1e30, gs, ALU.mult, ALU.add)
        vop("reduce_max", col8(g2), v3g(m2), AX.X)
        g2b = col8(g2).broadcast_to([128, 8, 4])
        vop("tensor_tensor", v3g(gm), v3g(gs), g2b, ALU.is_ge)
        gmb = v3g(gm).unsqueeze(3).broadcast_to([128, 8, 4, 8])
        vop("tensor_tensor", v4(x8), v4(sc), gmb, ALU.mult)
        vop("memset", sel, 0.0)
        for _ in range(8):
            vop("reduce_max", col8(mi), v3e(x8), AX.X)
            vop("tensor_single_scalar", gi, mi, 0.0, ALU.is_gt)
            mib = col8(mi).broadcast_to([128, 8, 32])
            vop("tensor_tensor", v3e(eqb), v3e(x8), mib, ALU.is_equal)
            gib = col8(gi).broadcast_to([128, 8, 32])
            vop("tensor_tensor", v3e(eqb), v3e(eqb), gib, ALU.mult)
            vop("tensor_add", sel, sel, eqb)
            vop("scalar_tensor_tensor", x8, eqb, -1e30, x8, ALU.mult, ALU.add)
        vop("tensor_mul", wraw, sctm, sel)
        vop("reduce_sum", col8(den), v3e(wraw), AX.X)
        vop("tensor_scalar_add", den, den, 1e-20)
        vop("reciprocal", rec, den)
        recb = col8(rec).broadcast_to([128, 8, 32])
        vop("scalar_tensor_tensor",
             v3e(S(MS_WTM, 512)[:, h * 256:h * 256 + 256]),
             v3e(wraw), 2.5, recb, ALU.mult, ALU.mult)
        p.di("v", sVE)
        if h == 0:
            p.wait("y", sSP, spill_v)
            xload_val[0] = p.dma(
                "y", sLD, S(X_OFF, 8192).rearrange("p (c t) -> p c t", c=8),
                xin[:, :].rearrange("(c p) t -> p c t", c=8, p=128)[:, :, TH:])
        if "rdbg" in tapd:
            p.wait("y", sVE)
            p.dma("y", sSP, tapd["rdbg"][0:128, 0:256], sctm)
            p.dma("y", sSP, tapd["rdbg"][0:128, 256:512], sc)
            p.dma("y", sSP, tapd["rdbg"][0:128, 512:768], sel)
            p.dma("y", sSP, tapd["rdbg"][0:128, 768:1024], wraw)
            p.dma("y", sSP, tapd["rdbg"][0:128, 1024:1056], gm)
            p.dma("y", sSP, tapd["rdbg"][0:128, 1056:1088], gs)
            p.dma("y", sSP, tapd["rdbg"][0:128, 1088:1096], den)
            p.dma("y", sSP, tapd["rdbg"][0:128, 1096:1104], g2)
            p.wait("y", sSP)
        if h == 1 and upto_i >= 11:
            emit_exp_load(p, slot_v_global, 0)
            emit_exp_load(p, slot_v_global, 1)
        if "wtm" in tapd and h == 1:
            p.wait("y", sVE)
            p.dma("y", sSP, tapd["wtm"][:, :], S(MS_WTM, 512))
            p.wait("y", sSP)
        if "x2n" in tapd and (h == 1 or upto_i == 5):
            p.wait("y", sVE)
            for c in range(8):
                p.dma("y", sSP, tapd["x2n"][c * 128:(c + 1) * 128, :],
                      S(X2N_OFF + c * 4096, 2048, BF16))
            p.wait("y", sSP)
        return p

    slot_v_global = {}

    def emit_exp_load(p, slot_v, x, wait_pe=None):
        slot_off = FSLOT_OFF + (x % 2) * 24576
        if wait_pe:
            p.wait("g", sPE, wait_pe)
        if x < 2:
            srcs = ((sg[:, x * 512:(x + 1) * 512], 8),
                    (su[:, x * 512:(x + 1) * 512], 8),
                    (sd[x * 512:(x + 1) * 512, :], 4))
        else:
            e = x - 2
            srcs = ((eg[e], 8), (eu[e], 8), (ed[e], 4))
        v = 0
        for wi, (src, kcn) in enumerate(srcs):
            v = p.dma("g", sWL,
                      S(slot_off + wi * 8192, 4096, BF16).rearrange(
                          "p (kc r) -> p kc r", kc=kcn),
                      src.rearrange("(kc p) r -> p kc r", kc=kcn, p=128))
        slot_v[x] = v

    def phase_F():
        p = phase()
        r2_reload_val[0] = p.dma(
            "y", sSP, S(R2_OFF, 8192).rearrange("p (c t) -> p c t", c=4),
            r2d[:, :].rearrange("(c p) t -> p c t", c=8, p=128)[:, 0:4, :])
        slot_v = {}
        if 0 in slot_v_global:
            slot_v.update(slot_v_global)
        else:
            emit_exp_load(p, slot_v, 0)
            emit_exp_load(p, slot_v, 1)
        peG = {}
        scG = {}
        veA = {}
        peD = {}
        veDadd = {}
        peDdone = {}
        for x in range(34):
            slot_off = FSLOT_OFF + (x % 2) * 24576
            p.wait("t", sWL, slot_v[x])
            for tt in range(4):
                lin = x * 4 + tt
                ab = S(APRIME_OFF + (lin % 2) * 4096, 2048, BF16)
                for ic in range(4):
                    L = lin * 4 + ic
                    if L >= 2:
                        p.wait("t", sSC, scG[L - 2])
                    gg = P(ic % 2)
                    gu = P(2 + ic % 2)
                    for kc in range(8):
                        p.op("t", "matmul", gg,
                             S(slot_off + kc * 1024 + ic * 256, 128, BF16),
                             X2Nc(kc, tt * 512), start=(kc == 0), stop=(kc == 7))
                    for kc in range(8):
                        p.op("t", "matmul", gu,
                             S(slot_off + 8192 + kc * 1024 + ic * 256, 128, BF16),
                             X2Nc(kc, tt * 512), start=(kc == 0), stop=(kc == 7))
                    peG[L] = p.di("t", sPE)
                    p.wait("s", sPE, peG[L])
                    if L >= 2:
                        p.wait("s", sVE, veA[L - 2])
                    p.op("s", "activation", S(SILU_OFF + (ic % 2) * 2048, 512, BF16),
                         gg, AF.Silu)
                    p.op("s", "copy",
                         S(SILU_OFF + 1024 + (ic % 2) * 2048, 512, BF16), gu)
                    scG[L] = p.di("s", sSC)
                    p.wait("v", sSC, scG[L])
                    if ic == 0 and lin >= 2:
                        p.wait("v", sPE, peDdone[lin - 2])
                    p.op("v", "tensor_mul", ab[:, ic * 512:(ic + 1) * 512],
                         S(SILU_OFF + (ic % 2) * 2048, 512, BF16),
                         S(SILU_OFF + 1024 + (ic % 2) * 2048, 512, BF16))
                    veA[L] = p.di("v", sVE)
                for ttok in range(4):
                    M = lin * 4 + ttok
                    dpar = ttok % 2
                    dps = psA[:, (4 + 2 * dpar) * 512:(4 + 2 * dpar) * 512 + 1024]
                    if M >= 2:
                        p.wait("t", sVE, veDadd[M - 2])
                    for ic in range(4):
                        p.wait("t", sVE, veA[lin * 4 + ic])
                        for hh in range(2):
                            p.op("t", "matmul", dps[:, hh * 512:(hh + 1) * 512],
                                 ab[:, ic * 512 + ttok * 128:
                                    ic * 512 + ttok * 128 + 128],
                                 S(slot_off + 16384 + ic * 2048 + hh * 1024,
                                   512, BF16),
                                 start=(ic == 0), stop=(ic == 3))
                    peD[M] = p.di("t", sPE)
                    tg = tt * 4 + ttok
                    p.wait("v", sPE, peD[M])
                    if x == 0:
                        p.op("v", "tensor_copy", FINt(tg), dps)
                    elif x == 1:
                        p.op("v", "scalar_tensor_tensor", FINt(tg), dps, 1.0,
                             FINt(tg), ALU.mult, ALU.add)
                    else:
                        p.op("v", "scalar_tensor_tensor", FINt(tg), dps,
                             S(MS_WTM, 512)[:, tg * 32 + (x - 2):tg * 32 + (x - 1)],
                             FINt(tg), ALU.mult, ALU.add)
                    veDadd[M] = p.di("v", sVE)
                peDdone[lin] = peD[lin * 4 + 3]
            if x + 2 < 34:
                emit_exp_load(p, slot_v, x + 2, wait_pe=peD[(x * 4 + 3) * 4 + 3])
        if "fin" in tapd:
            p.wait("y", sVE)
            p.dma("y", sSP, tapd["fin"][:, :], S(FIN_OFF, 16384))
            p.wait("y", sSP)
        return p

    def phase_H():
        p = phase()
        rv = p.dma("y", sSP,
                   S(R2_OFF + 32768, 8192).rearrange("p (c t) -> p c t", c=4),
                   r2d[:, :].rearrange("(c p) t -> p c t", c=8, p=128)[:, 4:8, :])
        hsum, hsq = S(MS_HSUM, 16), S(MS_HSQ, 16)
        hrstd, hneg = S(MS_HRSTD, 16), S(MS_HNEGMU, 16)
        for tg in range(16):
            p.op("s", "activation", S(HJUNK_OFF, 1024, BF16), FINt(tg), AF.Square,
                 accum_out=hsq[:, tg:tg + 1])
            p.op("s", "activation", S(HJUNK_OFF, 1024, BF16), FINt(tg), AF.Copy,
                 accum_out=hsum[:, tg:tg + 1])
        vsc = p.di("s", sSC)
        t0 = S(TMP_A, 512)[:, 0:16]
        t1 = S(TMP_B, 512)[:, 0:16]
        p.wait("v", sSC, vsc)
        p.op("v", "tensor_scalar_mul", hsum, hsum, 1.0 / 1024)
        p.op("v", "drain")
        p.op("v", "tensor_scalar_mul", hsq, hsq, 1.0 / 1024)
        p.op("v", "drain")
        p.op("v", "tensor_mul", t0, hsum, hsum)
        p.op("v", "drain")
        p.op("v", "tensor_sub", t1, hsq, t0)
        p.op("v", "tensor_scalar_mul", hneg, hsum, -1.0)
        vve = p.di("v", sVE)
        p.wait("s", sVE, vve)
        p.op("s", "activation", t0, t1, AF.Sqrt, bias=S(MS_EPS, 1))
        vsc = p.di("s", sSC)
        p.wait("v", sSC, vsc)
        p.op("v", "reciprocal", hrstd, t0)
        p.op("v", "drain")
        p.wait("v", sSP, rv)
        p.wait("t", sWL, vg_identbf)
        pe_tr = {}
        ve_out = {}
        for tg in range(16):
            zpar = tg % 2
            zb = S(ZBUF_OFF + zpar * 4096, 1024, F32)
            if tg >= 2:
                p.wait("v", sPE, pe_tr[tg - 2])
            p.op("v", "tensor_scalar", zb, FINt(tg), hneg[:, tg:tg + 1],
                 hrstd[:, tg:tg + 1], ALU.add, ALU.mult)
            ve_z = p.di("v", sVE)
            p.wait("t", sVE, ve_z)
            if tg >= 2:
                p.wait("t", sVE, ve_out[tg - 2])
            for c in range(8):
                trp = P(2 * zpar + c // 4, (c % 4) * 128, 128)
                p.op("t", "transpose", trp, zb[:, c * 128:(c + 1) * 128],
                     S(MS_IDENTF, 128))
            pe_tr[tg] = p.di("t", sPE)
            p.wait("v", sPE, pe_tr[tg])
            for half4 in range(2):
                trp = P(2 * zpar + half4, 0, 512).rearrange(
                    "p (c j) -> p c j", c=4)
                dst = S(R2_OFF, 16384).rearrange(
                    "p (c t) -> p c t", c=8)[:, half4 * 4:(half4 + 1) * 4,
                                             tg * 128:(tg + 1) * 128]
                p.op("v", "tensor_add", dst, dst, trp)
            ve_out[tg] = p.di("v", sVE)
            p.wait("y", sVE, ve_out[tg])
            p.dma("y", sSP,
                  ydram[:, :].rearrange("(c p) t -> p c t", c=8, p=128)[
                      :, :, tg * 128:(tg + 1) * 128],
                  S(R2_OFF, 16384).rearrange("p (c t) -> p c t", c=8)[
                      :, :, tg * 128:(tg + 1) * 128])
        if "hdbg" in tapd:
            p.wait("y", sVE)
            p.dma("y", sSP, tapd["hdbg"][:, 0:16], hsum)
            p.dma("y", sSP, tapd["hdbg"][:, 16:32], hsq)
            p.dma("y", sSP, tapd["hdbg"][:, 32:48], hrstd)
            p.dma("y", sSP, tapd["hdbg"][:, 48:64], hneg)
            p.wait("y", sSP)
        p.wait("y", sSP)
        return p

    spill0 = spill1 = 0
    if upto_i >= 1:
        phase_B(0)
    if upto_i >= 2:
        phase_C1(0)
    if upto_i >= 3:
        phase_C2(0)
    if upto_i >= 4:
        _, spill0 = phase_D(0)
    if upto_i >= 5:
        phase_E(0, spill0)
    if upto_i >= 6:
        phase_B(1)
    if upto_i >= 7:
        phase_C1(1)
    if upto_i >= 8:
        phase_C2(1)
    if upto_i >= 9:
        _, spill1 = phase_D(1)
    if upto_i >= 10:
        phase_E(1, spill1)
    if upto_i >= 11:
        phase_F()
    if upto_i >= 12:
        phase_H()

    for pr in progs:
        pr.replay_into(nc)

    arena_cm.__exit__(None, None, None)
    ps_cm.__exit__(None, None, None)
    return nc


def _prep_inputs(ins):
    f = np.float32
    x = np.asarray(ins["L0_attn_x"], f)
    base = {
        "wk": np.ascontiguousarray(np.asarray(ins["L0_attn_W_K"], f)),
        "wugg": np.ascontiguousarray(np.asarray(ins["L0_attn_W_ugg"], f)),
        "wout": np.ascontiguousarray(np.asarray(ins["L0_attn_W_out"], f)),
        "rms_col": np.ascontiguousarray(
            np.asarray(ins["L0_attn_rms_scale"], f).reshape(8, 128).T),
        "split_col": np.ascontiguousarray(
            np.asarray(ins["L0_attn_split_scale"], f).reshape(8, 128).T),
        "gwt": np.ascontiguousarray(np.asarray(ins["L0_ffn_gate_w"], f).T),
        "gbias": np.ascontiguousarray(
            np.asarray(ins["L0_ffn_gate_bias"], f).reshape(1, NE)),
        "eg": np.ascontiguousarray(
            np.asarray(ins["L0_ffn_expert_gate"], f).transpose(0, 2, 1)),
        "eu": np.ascontiguousarray(
            np.asarray(ins["L0_ffn_expert_up"], f).transpose(0, 2, 1)),
        "ed": np.ascontiguousarray(
            np.asarray(ins["L0_ffn_expert_down"], f).transpose(0, 2, 1)),
        "sg": np.ascontiguousarray(np.asarray(ins["L0_ffn_shared_gate"], f).T),
        "su": np.ascontiguousarray(np.asarray(ins["L0_ffn_shared_up"], f).T),
        "sd": np.ascontiguousarray(np.asarray(ins["L0_ffn_shared_down"], f).T),
        "osc": np.full((128, 128), 1.0 / 1024, f),
        "ident": np.eye(128, dtype=f),
        "ones1": np.ones((1, 128), f),
    }
    return [dict(base, xin=np.ascontiguousarray(x[b].T)) for b in range(NCORE)]


def kernel(**inputs):
    in_maps = _prep_inputs(inputs)
    nc = build_program()
    res = run_bass_kernel_spmd(nc, in_maps, core_ids=list(range(NCORE)))
    y = np.stack([r["y"] for r in res.results])
    return np.ascontiguousarray(y.transpose(0, 2, 1))


# version 07
# speedup vs baseline: 1.1318x; 1.1318x over previous
import numpy as np
import concourse.bass as bass
import concourse.mybir as mybir
from concourse.bass_utils import run_bass_kernel_spmd

F32 = mybir.dt.float32
BF16 = mybir.dt.bfloat16
U8 = mybir.dt.uint8
AF = mybir.ActivationFunctionType
ALU = mybir.AluOpType
AX = mybir.AxisListType

D = 1024
T = 2048
TH = 1024
NE = 32
NCORE = 8

X_OFF = 0
XN_OFF = 32768
X2N_OFF = 49152
HG_OFF = 81920
ATTN_OFF = 114688
SLABA_OFF = 131072
WOUT_OFF = 147456
BC_OFF = 180224
TMP_OFF = 188416
MISC_OFF = 201728
ARENA_BYTES = 209408
APRIME_OFF = 32768
SILU_OFF = 196608
FIN_OFF = 81920
FSLOT_OFF = 147456
R2_OFF = 0
ZBUF_OFF = BC_OFF
HJUNK_OFF = TMP_OFF
TMP_STAGE = TMP_OFF
TMP_A = TMP_OFF + 2304
TMP_B = TMP_OFF + 4352
TMP_C = TMP_OFF + 6400
TMP_D = TMP_OFF + 8448
TMP_E = TMP_OFF + 10496
MS_SQ0 = MISC_OFF
MS_SQ1 = MISC_OFF + 1024
MS_WTM = MISC_OFF + 2048
MS_KLAST = MISC_OFF + 4096
MS_CARRY = MISC_OFF + 4160
MS_SCOL = MISC_OFF + 4224
MS_RMS = MISC_OFF + 4256
MS_SPL = MISC_OFF + 4288
MS_BIASB = MISC_OFF + 4320
MS_IDENTBF = MISC_OFF + 4448
MS_OSCBF = MISC_OFF + 4704
MS_HSUM = MISC_OFF + 4960
MS_HSQ = MISC_OFF + 5024
MS_HRSTD = MISC_OFF + 5088
MS_HNEGMU = MISC_OFF + 5152
MS_OSCF = MISC_OFF + 5248
MS_IDENTF = MISC_OFF + 5760
MS_GWT = MISC_OFF + 6272
MS_ONES1 = MISC_OFF + 6784
MS_EPS = MISC_OFF + 7424
MS_GBIAS = MISC_OFF + 7296
E_SCF = SLABA_OFF
E_SCTM = SLABA_OFF + 4096
E_SC = SLABA_OFF + 5120
E_EQ = SLABA_OFF + 6144
E_SEL = SLABA_OFF + 7168
E_X8 = SLABA_OFF + 8192
E_WRAW = SLABA_OFF + 9216
E_M1 = SLABA_OFF + 10240
E_M2 = SLABA_OFF + 10368
E_GS = SLABA_OFF + 10496
E_G1 = SLABA_OFF + 10624
E_G2 = SLABA_OFF + 10656
E_GM = SLABA_OFF + 10688
E_MI = SLABA_OFF + 10816
E_GI = SLABA_OFF + 10848
E_DEN = SLABA_OFF + 10880
E_REC = SLABA_OFF + 10912
BC2_MU = SLABA_OFF + 11264
BC2_RSTD = SLABA_OFF + 13312

EB = {F32: 4, BF16: 2}

PH_ORDER = ["A", "B0", "C10", "C20", "D0", "E0",
            "B1", "C11", "C21", "D1", "E1", "F", "H"]


class Sem:
    def __init__(self, nc, name):
        self.h = nc.alloc_semaphore(name=name)
        self.v = 0


def _replay(eng, q):
    for it in q:
        k = it[0]
        if k == "w":
            eng.wait_ge(it[1], it[2])
        elif k == "o":
            getattr(eng, it[1])(*it[2], **it[3])
        elif k == "d":
            eng.dma_start(it[1], it[2]).then_inc(it[3], 16)
        else:
            eng.drain().then_inc(it[1], it[2])


class Prog:
    def __init__(self):
        self.q = {k: [] for k in ("t", "v", "s", "g", "y")}
        self.hwm = {k: {} for k in ("t", "v", "s", "g", "y")}

    def wait(self, e, sem, val=None):
        v = sem.v if val is None else val
        if v > 0 and v > self.hwm[e].get(id(sem), -1):
            self.hwm[e][id(sem)] = v
            self.q[e].append(("w", sem.h, v))

    def op(self, e, meth, *args, **kw):
        self.q[e].append(("o", meth, args, kw))

    def dma(self, e, sem, out, in_):
        sem.v += 16
        self.q[e].append(("d", out, in_, sem.h))
        return sem.v

    def di(self, e, sem, n=1):
        sem.v += n
        self.q[e].append(("i", sem.h, n))
        return sem.v

    def replay_into(self, nc):
        with nc.Block() as blk:
            if self.q["t"]:
                @blk.tensor
                def _(eng, q=self.q["t"]):
                    _replay(eng, q)
            if self.q["v"]:
                @blk.vector
                def _(eng, q=self.q["v"]):
                    _replay(eng, q)
            if self.q["s"]:
                @blk.scalar
                def _(eng, q=self.q["s"]):
                    _replay(eng, q)
            if self.q["g"]:
                @blk.gpsimd
                def _(eng, q=self.q["g"]):
                    _replay(eng, q)
            if self.q["y"]:
                @blk.sync
                def _(eng, q=self.q["y"]):
                    _replay(eng, q)


def build_program(debug_taps=(), upto="H"):
    upto_i = PH_ORDER.index(upto)
    nc = bass.Bass("TRN2", target_bir_lowering=False)

    xin = nc.dram_tensor("xin", [D, T], F32, kind="ExternalInput")
    wk = nc.dram_tensor("wk", [D, 2048], F32, kind="ExternalInput")
    wugg = nc.dram_tensor("wugg", [D, 6144], F32, kind="ExternalInput")
    wout = nc.dram_tensor("wout", [2048, D], F32, kind="ExternalInput")
    rms_col = nc.dram_tensor("rms_col", [128, 8], F32, kind="ExternalInput")
    split_col = nc.dram_tensor("split_col", [128, 8], F32, kind="ExternalInput")
    gwt = nc.dram_tensor("gwt", [D, NE], F32, kind="ExternalInput")
    gbias = nc.dram_tensor("gbias", [1, NE], F32, kind="ExternalInput")
    eg = nc.dram_tensor("eg", [NE, D, 512], F32, kind="ExternalInput")
    eu = nc.dram_tensor("eu", [NE, D, 512], F32, kind="ExternalInput")
    ed = nc.dram_tensor("ed", [NE, 512, D], F32, kind="ExternalInput")
    sg = nc.dram_tensor("sg", [D, D], F32, kind="ExternalInput")
    su = nc.dram_tensor("su", [D, D], F32, kind="ExternalInput")
    sd = nc.dram_tensor("sd", [D, D], F32, kind="ExternalInput")
    osc = nc.dram_tensor("osc", [128, 128], F32, kind="ExternalInput")
    ident = nc.dram_tensor("ident", [128, 128], F32, kind="ExternalInput")
    ones1 = nc.dram_tensor("ones1", [1, 128], F32, kind="ExternalInput")
    ydram = nc.dram_tensor("y", [D, T], F32, kind="ExternalOutput")
    r2d = nc.dram_tensor("r2d", [D, T], F32,
                         kind="ExternalOutput" if "r2" in debug_taps else "Internal")
    tapd = {}
    for name, shape, dt in (("xn", [D, TH], BF16), ("hg", [2048, TH], BF16),
                            ("attn", [D, TH], BF16), ("x2n", [D, T], BF16),
                            ("wtm", [128, 512], F32), ("fin", [128, 16384], F32),
                            ("scores", [32, TH], F32), ("rdbg", [128, 1104], F32),
                            ("hdbg", [128, 64], F32)):
        if name in debug_taps:
            tapd[name] = nc.dram_tensor("tap_" + name, shape, dt, kind="ExternalOutput")

    arena_cm = nc.sbuf_tensor("arena", [128, ARENA_BYTES], U8)
    ps_cm = nc.psum_tensor("ps", [128, 4096], F32)
    arena = arena_cm.__enter__()
    psA = ps_cm.__enter__()

    def S(off, cols, dt=F32, parts=128, p0=0):
        return arena[p0:p0 + parts, off:off + cols * EB[dt]].bitcast(dt)

    def P(bank, off=0, cols=512, parts=128):
        return psA[0:parts, bank * 512 + off: bank * 512 + off + cols]

    sLD = Sem(nc, "sLD")
    sWL = Sem(nc, "sWL")
    sSP = Sem(nc, "sSP")
    sPE = Sem(nc, "sPE")
    sSC = Sem(nc, "sSC")
    sVE = Sem(nc, "sVE")

    progs = []

    def phase():
        p = Prog()
        progs.append(p)
        return p

    def Xc(c, tc, n=512):
        return S(X_OFF + c * 4096, 1024)[:, tc:tc + n]

    def XNc(kc, tc, n=512):
        return S(XN_OFF + kc * 2048, 1024, BF16)[:, tc:tc + n]

    def X2Nc(kc, col, n=512):
        return S(X2N_OFF + kc * 4096, 2048, BF16)[:, col:col + n]

    def HGc(hc, tc, n=512):
        return S(HG_OFF + hc * 2048, 1024, BF16)[:, tc:tc + n]

    def ATTNc(c, tc, n=512):
        return S(ATTN_OFF + c * 2048, 1024, BF16)[:, tc:tc + n]

    def FINt(tg):
        return S(FIN_OFF + tg * 4096, 1024)

    ones1_sb = S(MS_ONES1, 128, F32, parts=1)
    gbias_sb = S(MS_GBIAS, 32, F32, parts=1)
    xload_val = [0]
    r2_reload_val = [0]
    wl_slot = {}

    def emit_c1_slot_load(p, hc, wait_pe_val=None):
        slot_off = SLABA_OFF + (hc % 2) * 8192
        if wait_pe_val:
            p.wait("g", sPE, wait_pe_val)
        v = 0
        for wi, (src, col0) in enumerate(
                ((wk, hc * 128), (wugg, hc * 128),
                 (wugg, 2048 + hc * 128), (wugg, 4096 + hc * 128))):
            v = p.dma("g", sWL,
                      S(slot_off + wi * 2048, 1024, BF16).rearrange(
                          "p (kc j) -> p kc j", kc=8),
                      src[:, col0:col0 + 128].rearrange(
                          "(kc p) j -> p kc j", kc=8, p=128))
        wl_slot[hc] = v

    pA = phase()
    va_x = pA.dma("y", sLD, S(X_OFF, 8192).rearrange("p (c t) -> p c t", c=8),
                  xin[:, :].rearrange("(c p) t -> p c t", c=8, p=128)[:, :, 0:TH])
    pA.dma("y", sLD, S(MS_RMS, 8), rms_col[:, :])
    pA.dma("y", sLD, S(MS_SPL, 8), split_col[:, :])
    va_oscf = pA.dma("y", sLD, S(MS_OSCF, 128), osc[:, :])
    va_identf = pA.dma("y", sLD, S(MS_IDENTF, 128), ident[:, :])
    pA.dma("y", sLD, ones1_sb, ones1[:, :])
    va_small = pA.dma("y", sLD, gbias_sb, gbias[:, :])
    vg_identbf = pA.dma("g", sWL, S(MS_IDENTBF, 128, BF16), ident[:, :])
    vg_oscbf = pA.dma("g", sWL, S(MS_OSCBF, 128, BF16), osc[:, :])
    vg_gwt = pA.dma("g", sWL,
                    S(MS_GWT, 256, BF16).rearrange("p (kc e) -> p kc e", kc=8),
                    gwt[:, :].rearrange("(kc p) e -> p kc e", kc=8, p=128))
    vg_wout = pA.dma("g", sWL,
                     S(WOUT_OFF, 16384, BF16).rearrange("p (kc o) -> p kc o", kc=16),
                     wout[:, :].rearrange("(kc p) o -> p kc o", kc=16, p=128))
    emit_c1_slot_load(pA, 0)
    emit_c1_slot_load(pA, 1)
    pA.op("v", "memset", S(MS_KLAST, 16), 0.0)
    pA.op("v", "memset", S(MS_EPS, 1), 1e-5)
    pA.op("v", "memset", S(MS_CARRY, 16), 0.0)
    pA.wait("v", sLD, va_small)
    pA.op("v", "tensor_mul", S(MS_SCOL, 8), S(MS_RMS, 8), S(MS_SPL, 8))
    pA.di("v", sVE)

    def phase_B(h):
        p = phase()
        xv = va_x if h == 0 else xload_val[0]
        p.wait("t", sLD, xv)
        p.wait("t", sLD, va_oscf)
        p.wait("t", sWL, vg_oscbf)
        p.wait("s", sLD, xv)
        scol = S(MS_SCOL, 8)
        vpe_prev = 0
        ve_end_prev = 0
        for tt in range(2):
            tc = tt * 512
            mu_ps = P(4 * tt + 0)
            sq_ps = P(4 * tt + 1)
            sqv = []
            for c in range(8):
                if c == 0 and vpe_prev:
                    p.wait("s", sPE, vpe_prev)
                    p.wait("s", sVE, ve_end_prev)
                p.op("s", "activation", S(HG_OFF + c * 1024, 512, BF16),
                     Xc(c, tc), AF.Square)
                sqv.append(p.di("s", sSC))
            for c in range(8):
                p.op("t", "matmul", mu_ps, S(MS_OSCF, 128), Xc(c, tc),
                     start=(c == 0), stop=(c == 7))
            for c in range(8):
                p.wait("t", sSC, sqv[c])
                p.op("t", "matmul", sq_ps, S(MS_OSCBF, 128, BF16),
                     S(HG_OFF + c * 1024, 512, BF16),
                     start=(c == 0), stop=(c == 7))
            vpe = p.di("t", sPE)
            vpe_prev = vpe
            t0, t1, t2, t3, t4, t5 = (S(TMP_A, 512), S(TMP_B, 512), S(TMP_C, 512),
                                      S(TMP_D, 512), S(TMP_E, 512),
                                      S(TMP_STAGE, 512))
            p.wait("s", sPE, vpe)
            p.op("s", "activation", t0, mu_ps, AF.Square)
            p.op("s", "copy", S(BC_OFF, 1024)[:, tc:tc + 512], mu_ps)
            vsq = p.di("s", sSC)
            p.wait("v", sPE, vpe)
            p.wait("v", sSC, vsq)
            p.op("v", "tensor_sub", t1, sq_ps, t0)
            vve = p.di("v", sVE)
            p.wait("s", sVE, vve)
            p.op("s", "activation", t2, t1, AF.Sqrt, bias=S(MS_EPS, 1))
            p.op("s", "activation", t3, t1, AF.Sqrt)
            vsc = p.di("s", sSC)
            p.wait("v", sSC, vsc)
            p.op("v", "reciprocal", S(BC_OFF + 4096, 1024)[:, tc:tc + 512], t2)
            p.op("v", "scalar_tensor_tensor", t0, t2, 1e-8, t3, ALU.mult, ALU.add)
            p.op("v", "reciprocal", t4, t0)
            for c in range(8):
                p.op("v", "tensor_sub", t5, Xc(c, tc), mu_ps)
                p.op("v", "scalar_tensor_tensor", XNc(c, tc), t5,
                     scol[:, c:c + 1], t4, ALU.mult, ALU.mult)
            ve_end_prev = p.di("v", sVE)
        if "xn" in tapd and h == 0:
            p.wait("y", sVE)
            for c in range(8):
                p.dma("y", sSP, tapd["xn"][c * 128:(c + 1) * 128, :],
                      S(XN_OFF + c * 2048, 1024, BF16))
            p.wait("y", sSP)
        return p

    def phase_C1(h):
        p = phase()
        if h == 1:
            emit_c1_slot_load(p, 0)
            emit_c1_slot_load(p, 1)
        sc_done = {}
        ve_done = {}
        pe_done = {}
        for hc in range(16):
            slot_off = SLABA_OFF + (hc % 2) * 8192
            for tt in range(2):
                k = hc * 2 + tt
                par = k % 2
                pk, pgi, pu, pgo = (P(4 * par + 0), P(4 * par + 1),
                                    P(4 * par + 2), P(4 * par + 3))
                p.wait("t", sWL, wl_slot[hc])
                if k >= 2:
                    p.wait("t", sSC, sc_done[k - 2])
                    p.wait("t", sVE, ve_done[k - 2])
                for wi, dst in ((0, pk), (2, pgi), (1, pu), (3, pgo)):
                    for kc in range(8):
                        p.op("t", "matmul", dst,
                             S(slot_off + wi * 2048 + kc * 256, 128, BF16),
                             XNc(kc, tt * 512), start=(kc == 0), stop=(kc == 7))
                pe_done[k] = p.di("t", sPE)
                stage = S(TMP_STAGE, 514)
                tmKm, tmGi, tmGo, tmU, tmH = (S(TMP_A, 512), S(TMP_B, 512),
                                              S(TMP_C, 512), S(TMP_D, 512),
                                              S(TMP_E, 512))
                p.wait("s", sPE, pe_done[k])
                if k >= 1:
                    p.wait("s", sVE, ve_done[k - 1])
                p.op("s", "copy", stage[:, 0:1], S(MS_KLAST, 16)[:, hc:hc + 1])
                p.op("s", "activation", stage[:, 1:513], pk, AF.Sigmoid)
                p.op("s", "drain")
                p.op("s", "copy", S(MS_KLAST, 16)[:, hc:hc + 1], stage[:, 512:513])
                p.op("s", "activation", tmKm, pk, AF.Sigmoid, scale=-1.0)
                p.op("s", "activation", tmGi, pgi, AF.Sigmoid)
                p.op("s", "activation", tmGo, pgo, AF.Sigmoid)
                sc_done[k] = p.di("s", sSC)
                p.wait("v", sSC, sc_done[k])
                p.op("v", "tensor_mul", tmGi, tmGi, tmKm)
                p.op("v", "tensor_mul", tmU, pu, tmGi)
                p.op("v", "tensor_tensor_scan", tmH, stage[:, 0:512], tmU,
                     S(MS_CARRY, 16)[:, hc:hc + 1], ALU.mult, ALU.add)
                p.op("v", "drain")
                p.op("v", "tensor_copy", S(MS_CARRY, 16)[:, hc:hc + 1],
                     tmH[:, 511:512])
                p.op("v", "tensor_mul", HGc(hc, tt * 512), tmH, tmGo)
                ve_done[k] = p.di("v", sVE)
            if hc + 2 < 16:
                emit_c1_slot_load(p, hc + 2, wait_pe_val=pe_done[hc * 2 + 1])
        if "hg" in tapd and h == 0:
            p.wait("y", sVE)
            for hc in range(16):
                p.dma("y", sSP, tapd["hg"][hc * 128:(hc + 1) * 128, :],
                      S(HG_OFF + hc * 2048, 1024, BF16))
            p.wait("y", sSP)
        return p

    def phase_C2(h):
        p = phase()
        p.wait("t", sWL, vg_wout)
        ve2 = {}
        for tt in range(2):
            tc = tt * 512
            for seth in range(2):
                m = tt * 2 + seth
                par = m % 2
                if m >= 2:
                    p.wait("t", sVE, ve2[m - 2])
                for oc in range(4):
                    c = seth * 4 + oc
                    dst = P(4 * par + oc)
                    for kc in range(16):
                        p.op("t", "matmul", dst,
                             S(WOUT_OFF + kc * 2048 + c * 256, 128, BF16),
                             HGc(kc, tc), start=(kc == 0), stop=(kc == 15))
                vpe = p.di("t", sPE)
                p.wait("v", sPE, vpe)
                for oc in range(4):
                    c = seth * 4 + oc
                    tmp0 = S(TMP_A, 512)
                    p.op("v", "tensor_sub", tmp0, Xc(c, tc),
                         S(BC_OFF, 1024)[:, tc:tc + 512])
                    p.op("v", "tensor_mul", tmp0, tmp0,
                         S(BC_OFF + 4096, 1024)[:, tc:tc + 512])
                    p.op("v", "tensor_add", ATTNc(c, tc), tmp0, P(4 * par + oc))
                ve2[m] = p.di("v", sVE)
        if "attn" in tapd and h == 0:
            p.wait("y", sVE)
            for c in range(8):
                p.dma("y", sSP, tapd["attn"][c * 128:(c + 1) * 128, :],
                      S(ATTN_OFF + c * 2048, 1024, BF16))
            p.wait("y", sSP)
        return p

    def phase_D(h):
        p = phase()
        p.wait("t", sLD, va_oscf)
        vpe3_prev = 0
        ve_end_prev = 0
        for tt in range(2):
            tc = tt * 512
            mu_ps = P(4 * tt + 0)
            sq_ps = P(4 * tt + 1)
            sqv = []
            for c in range(8):
                if c == 0 and vpe3_prev:
                    p.wait("s", sPE, vpe3_prev)
                    p.wait("s", sVE, ve_end_prev)
                p.op("s", "activation", S(HG_OFF + c * 1024, 512, BF16),
                     ATTNc(c, tc), AF.Square)
                sqv.append(p.di("s", sSC))
            for c in range(8):
                p.op("t", "matmul", mu_ps, S(MS_OSCBF, 128, BF16), ATTNc(c, tc),
                     start=(c == 0), stop=(c == 7))
            for c in range(8):
                p.wait("t", sSC, sqv[c])
                p.op("t", "matmul", sq_ps, S(MS_OSCBF, 128, BF16),
                     S(HG_OFF + c * 1024, 512, BF16),
                     start=(c == 0), stop=(c == 7))
            vpe = p.di("t", sPE)
            t0, t1, t2 = S(TMP_A, 512), S(TMP_B, 512), S(TMP_C, 512)
            p.wait("s", sPE, vpe)
            p.op("s", "activation", t0, mu_ps, AF.Square)
            p.op("s", "copy", S(BC2_MU, 512), mu_ps)
            vsq = p.di("s", sSC)
            p.wait("v", sPE, vpe)
            p.wait("v", sSC, vsq)
            p.op("v", "tensor_sub", t1, sq_ps, t0)
            vve = p.di("v", sVE)
            p.wait("s", sVE, vve)
            p.op("s", "activation", t2, t1, AF.Sqrt, bias=S(MS_EPS, 1))
            vsc = p.di("s", sSC)
            p.wait("v", sSC, vsc)
            p.op("v", "reciprocal", S(BC2_RSTD, 512), t2)
            for c in range(8):
                tmp0 = S(TMP_D, 512)
                p.op("v", "tensor_sub", tmp0, ATTNc(c, tc), S(BC2_MU, 512))
                p.op("v", "tensor_mul", tmp0, tmp0, S(BC2_RSTD, 512))
                p.op("v", "tensor_add", Xc(c, tc), tmp0, Xc(c, tc))
            vd2 = p.di("v", sVE)
            mu3_ps = P(4 * tt + 2)
            sq3_ps = P(4 * tt + 3)
            sq3 = []
            p.wait("s", sVE, vd2)
            for c in range(8):
                p.op("s", "activation", S(HG_OFF + 8192 + c * 1024, 512, BF16),
                     Xc(c, tc), AF.Square)
                sq3.append(p.di("s", sSC))
            p.wait("t", sVE, vd2)
            for c in range(8):
                p.op("t", "matmul", mu3_ps, S(MS_OSCF, 128), Xc(c, tc),
                     start=(c == 0), stop=(c == 7))
            for c in range(8):
                p.wait("t", sSC, sq3[c])
                p.op("t", "matmul", sq3_ps, S(MS_OSCBF, 128, BF16),
                     S(HG_OFF + 8192 + c * 1024, 512, BF16),
                     start=(c == 0), stop=(c == 7))
            vpe3 = p.di("t", sPE)
            vpe3_prev = vpe3
            p.wait("s", sPE, vpe3)
            p.op("s", "activation", t0, mu3_ps, AF.Square)
            vsq3 = p.di("s", sSC)
            p.wait("v", sPE, vpe3)
            p.wait("v", sSC, vsq3)
            p.op("v", "tensor_sub", t1, sq3_ps, t0)
            vve3 = p.di("v", sVE)
            p.wait("s", sVE, vve3)
            p.op("s", "activation", t2, t1, AF.Sqrt, bias=S(MS_EPS, 1))
            vsc3 = p.di("s", sSC)
            p.wait("v", sSC, vsc3)
            p.op("v", "reciprocal", t0, t2)
            for c in range(8):
                tmp0 = S(TMP_D, 512)
                p.op("v", "tensor_sub", tmp0, Xc(c, tc), mu3_ps)
                p.op("v", "tensor_mul", X2Nc(c, h * 1024 + tc), tmp0, t0)
            ve_end_prev = p.di("v", sVE)
        p.wait("y", sVE)
        spill_v = p.dma(
            "y", sSP,
            r2d[:, :].rearrange("(c p) t -> p c t", c=8, p=128)[:, :, h * TH:(h + 1) * TH],
            S(X_OFF, 8192).rearrange("p (c t) -> p c t", c=8))
        return p, spill_v

    def phase_E(h, spill_v):
        p = phase()
        scf = S(E_SCF, 1024, F32, parts=32)
        p.wait("t", sWL, vg_gwt)
        vs_prev = 0
        for tt in range(2):
            scps = P(0, 0, 512, parts=32)
            if vs_prev:
                p.wait("t", sSC, vs_prev)
            for kc in range(8):
                p.op("t", "matmul", scps, S(MS_GWT + kc * 64, 32, BF16),
                     X2Nc(kc, h * 1024 + tt * 512), start=(kc == 0), stop=(kc == 7))
            vpe = p.di("t", sPE)
            p.wait("s", sPE, vpe)
            p.op("s", "activation", scf[:, tt * 512:(tt + 1) * 512], scps, AF.Sigmoid)
            vs_prev = p.di("s", sSC)
        if "scores" in tapd and h == 0:
            p.wait("y", sSC)
            p.dma("y", sSP, tapd["scores"][:, :], scf)
            p.wait("y", sSP)
        p.wait("t", sLD, va_identf)
        sctm = S(E_SCTM, 256)
        vtr = {}
        for tk in range(8):
            trps = P(2 + tk % 2, 0, 32)
            if tk >= 2:
                p.wait("t", sSC, vtr[tk - 2])
            p.op("t", "transpose", trps, scf[:, tk * 128:(tk + 1) * 128],
                 S(MS_IDENTF, 128)[0:32, 0:32])
            vpe = p.di("t", sPE)
            p.wait("s", sPE, vpe)
            p.op("s", "copy", sctm[:, tk * 32:(tk + 1) * 32], trps)
            vtr[tk] = p.di("s", sSC)
        if h == 0:
            p.wait("t", sLD, va_small)
            bb = P(4, 0, 32)
            p.op("t", "matmul", bb, ones1_sb, gbias_sb, start=True, stop=True)
            vpe = p.di("t", sPE)
            p.wait("s", sPE, vpe)
            p.op("s", "copy", S(MS_BIASB, 32), bb)
            p.di("s", sSC)
        p.wait("v", sSC)
        sc, eqb, sel, x8, wraw = (S(E_SC, 256), S(E_EQ, 256), S(E_SEL, 256),
                                  S(E_X8, 256), S(E_WRAW, 256))
        m1, m2, gs = S(E_M1, 32), S(E_M2, 32), S(E_GS, 32)
        g1, g2 = S(E_G1, 8), S(E_G2, 8)
        gm, mi, gi = S(E_GM, 32), S(E_MI, 8), S(E_GI, 8)
        den, rec = S(E_DEN, 8), S(E_REC, 8)

        def v4(ap):
            return ap.rearrange("p (t g e) -> p t g e", t=8, g=4, e=8)

        def v3g(ap):
            return ap.rearrange("p (t g) -> p t g", t=8, g=4)

        def v3e(ap):
            return ap.rearrange("p (t e) -> p t e", t=8, e=32)

        def col8(ap):
            return ap.rearrange("p (t o) -> p t o", t=8)

        bias_view = S(MS_BIASB, 32).rearrange("p (o e) -> p o e", o=1) \
            .broadcast_to([128, 8, 32])

        def vop(meth, *args, **kw):
            p.op("v", meth, *args, **kw)
            p.op("v", "drain")

        vop("tensor_tensor", v3e(sc), v3e(sctm), bias_view, ALU.add)
        vop("reduce_max", v3g(m1), v4(sc), AX.X)
        m1b = v3g(m1).unsqueeze(3).broadcast_to([128, 8, 4, 8])
        vop("tensor_tensor", v4(eqb), v4(sc), m1b, ALU.is_equal)
        vop("scalar_tensor_tensor", x8, eqb, -1e30, sc, ALU.mult, ALU.add)
        vop("reduce_max", v3g(m2), v4(x8), AX.X)
        vop("tensor_add", gs, m1, m2)
        vop("reduce_max", col8(g1), v3g(gs), AX.X)
        g1b = col8(g1).broadcast_to([128, 8, 4])
        vop("tensor_tensor", v3g(m1), v3g(gs), g1b, ALU.is_equal)
        vop("scalar_tensor_tensor", m2, m1, -1e30, gs, ALU.mult, ALU.add)
        vop("reduce_max", col8(g2), v3g(m2), AX.X)
        g2b = col8(g2).broadcast_to([128, 8, 4])
        vop("tensor_tensor", v3g(gm), v3g(gs), g2b, ALU.is_ge)
        gmb = v3g(gm).unsqueeze(3).broadcast_to([128, 8, 4, 8])
        vop("tensor_tensor", v4(x8), v4(sc), gmb, ALU.mult)
        vop("memset", sel, 0.0)
        for _ in range(8):
            vop("reduce_max", col8(mi), v3e(x8), AX.X)
            vop("tensor_single_scalar", gi, mi, 0.0, ALU.is_gt)
            mib = col8(mi).broadcast_to([128, 8, 32])
            vop("tensor_tensor", v3e(eqb), v3e(x8), mib, ALU.is_equal)
            gib = col8(gi).broadcast_to([128, 8, 32])
            vop("tensor_tensor", v3e(eqb), v3e(eqb), gib, ALU.mult)
            vop("tensor_add", sel, sel, eqb)
            vop("scalar_tensor_tensor", x8, eqb, -1e30, x8, ALU.mult, ALU.add)
        vop("tensor_mul", wraw, sctm, sel)
        vop("reduce_sum", col8(den), v3e(wraw), AX.X)
        vop("tensor_scalar_add", den, den, 1e-20)
        vop("reciprocal", rec, den)
        recb = col8(rec).broadcast_to([128, 8, 32])
        vop("scalar_tensor_tensor",
             v3e(S(MS_WTM, 512)[:, h * 256:h * 256 + 256]),
             v3e(wraw), 2.5, recb, ALU.mult, ALU.mult)
        p.di("v", sVE)
        if h == 0:
            p.wait("y", sSP, spill_v)
            xload_val[0] = p.dma(
                "y", sLD, S(X_OFF, 8192).rearrange("p (c t) -> p c t", c=8),
                xin[:, :].rearrange("(c p) t -> p c t", c=8, p=128)[:, :, TH:])
        if "rdbg" in tapd:
            p.wait("y", sVE)
            p.dma("y", sSP, tapd["rdbg"][0:128, 0:256], sctm)
            p.dma("y", sSP, tapd["rdbg"][0:128, 256:512], sc)
            p.dma("y", sSP, tapd["rdbg"][0:128, 512:768], sel)
            p.dma("y", sSP, tapd["rdbg"][0:128, 768:1024], wraw)
            p.dma("y", sSP, tapd["rdbg"][0:128, 1024:1056], gm)
            p.dma("y", sSP, tapd["rdbg"][0:128, 1056:1088], gs)
            p.dma("y", sSP, tapd["rdbg"][0:128, 1088:1096], den)
            p.dma("y", sSP, tapd["rdbg"][0:128, 1096:1104], g2)
            p.wait("y", sSP)
        if h == 1 and upto_i >= 11:
            emit_exp_load(p, slot_v_global, 0)
            emit_exp_load(p, slot_v_global, 1)
        if "wtm" in tapd and h == 1:
            p.wait("y", sVE)
            p.dma("y", sSP, tapd["wtm"][:, :], S(MS_WTM, 512))
            p.wait("y", sSP)
        if "x2n" in tapd and (h == 1 or upto_i == 5):
            p.wait("y", sVE)
            for c in range(8):
                p.dma("y", sSP, tapd["x2n"][c * 128:(c + 1) * 128, :],
                      S(X2N_OFF + c * 4096, 2048, BF16))
            p.wait("y", sSP)
        return p

    slot_v_global = {}

    def emit_exp_load(p, slot_v, x, wait_pe=None):
        slot_off = FSLOT_OFF + (x % 2) * 24576
        if wait_pe:
            p.wait("g", sPE, wait_pe)
        if x < 2:
            srcs = ((sg[:, x * 512:(x + 1) * 512], 8),
                    (su[:, x * 512:(x + 1) * 512], 8),
                    (sd[x * 512:(x + 1) * 512, :], 4))
        else:
            e = x - 2
            srcs = ((eg[e], 8), (eu[e], 8), (ed[e], 4))
        v = 0
        for wi, (src, kcn) in enumerate(srcs):
            v = p.dma("g", sWL,
                      S(slot_off + wi * 8192, 4096, BF16).rearrange(
                          "p (kc r) -> p kc r", kc=kcn),
                      src.rearrange("(kc p) r -> p kc r", kc=kcn, p=128))
        slot_v[x] = v

    def phase_F():
        p = phase()
        r2_reload_val[0] = p.dma(
            "y", sSP, S(R2_OFF, 8192).rearrange("p (c t) -> p c t", c=4),
            r2d[:, :].rearrange("(c p) t -> p c t", c=8, p=128)[:, 0:4, :])
        slot_v = {}
        if 0 in slot_v_global:
            slot_v.update(slot_v_global)
        else:
            emit_exp_load(p, slot_v, 0)
            emit_exp_load(p, slot_v, 1)
        peG = {}
        scG = {}
        veA = {}
        peD = {}
        veDadd = {}
        peDdone = {}
        for x in range(34):
            slot_off = FSLOT_OFF + (x % 2) * 24576
            p.wait("t", sWL, slot_v[x])
            for tt in range(4):
                lin = x * 4 + tt
                ab = S(APRIME_OFF + (lin % 2) * 4096, 2048, BF16)
                for ic in range(4):
                    L = lin * 4 + ic
                    if L >= 2:
                        p.wait("t", sSC, scG[L - 2])
                    gg = P(ic % 2)
                    gu = P(2 + ic % 2)
                    for kc in range(8):
                        p.op("t", "matmul", gg,
                             S(slot_off + kc * 1024 + ic * 256, 128, BF16),
                             X2Nc(kc, tt * 512), start=(kc == 0), stop=(kc == 7))
                    for kc in range(8):
                        p.op("t", "matmul", gu,
                             S(slot_off + 8192 + kc * 1024 + ic * 256, 128, BF16),
                             X2Nc(kc, tt * 512), start=(kc == 0), stop=(kc == 7))
                    peG[L] = p.di("t", sPE)
                    p.wait("s", sPE, peG[L])
                    if L >= 2:
                        p.wait("s", sVE, veA[L - 2])
                    p.op("s", "activation", S(SILU_OFF + (ic % 2) * 2048, 512, BF16),
                         gg, AF.Silu)
                    p.op("s", "copy",
                         S(SILU_OFF + 1024 + (ic % 2) * 2048, 512, BF16), gu)
                    scG[L] = p.di("s", sSC)
                    p.wait("v", sSC, scG[L])
                    if ic == 0 and lin >= 2:
                        p.wait("v", sPE, peDdone[lin - 2])
                    p.op("v", "tensor_mul", ab[:, ic * 512:(ic + 1) * 512],
                         S(SILU_OFF + (ic % 2) * 2048, 512, BF16),
                         S(SILU_OFF + 1024 + (ic % 2) * 2048, 512, BF16))
                    veA[L] = p.di("v", sVE)
                for ttok in range(4):
                    M = lin * 4 + ttok
                    dpar = ttok % 2
                    dps = psA[:, (4 + 2 * dpar) * 512:(4 + 2 * dpar) * 512 + 1024]
                    if M >= 2:
                        p.wait("t", sVE, veDadd[M - 2])
                    for hh in range(2):
                        for ic in range(4):
                            p.wait("t", sVE, veA[lin * 4 + ic])
                            p.op("t", "matmul", dps[:, hh * 512:(hh + 1) * 512],
                                 ab[:, ic * 512 + ttok * 128:
                                    ic * 512 + ttok * 128 + 128],
                                 S(slot_off + 16384 + ic * 2048 + hh * 1024,
                                   512, BF16),
                                 start=(ic == 0), stop=(ic == 3))
                    peD[M] = p.di("t", sPE)
                    tg = tt * 4 + ttok
                    p.wait("v", sPE, peD[M])
                    if x == 0:
                        p.op("v", "tensor_copy", FINt(tg), dps)
                    elif x == 1:
                        p.op("v", "scalar_tensor_tensor", FINt(tg), dps, 1.0,
                             FINt(tg), ALU.mult, ALU.add)
                    else:
                        p.op("v", "scalar_tensor_tensor", FINt(tg), dps,
                             S(MS_WTM, 512)[:, tg * 32 + (x - 2):tg * 32 + (x - 1)],
                             FINt(tg), ALU.mult, ALU.add)
                    veDadd[M] = p.di("v", sVE)
                peDdone[lin] = peD[lin * 4 + 3]
            if x + 2 < 34:
                emit_exp_load(p, slot_v, x + 2, wait_pe=peD[(x * 4 + 3) * 4 + 3])
        if "fin" in tapd:
            p.wait("y", sVE)
            p.dma("y", sSP, tapd["fin"][:, :], S(FIN_OFF, 16384))
            p.wait("y", sSP)
        return p

    def phase_H():
        p = phase()
        rv = p.dma("y", sSP,
                   S(R2_OFF + 32768, 8192).rearrange("p (c t) -> p c t", c=4),
                   r2d[:, :].rearrange("(c p) t -> p c t", c=8, p=128)[:, 4:8, :])
        hsum, hsq = S(MS_HSUM, 16), S(MS_HSQ, 16)
        hrstd, hneg = S(MS_HRSTD, 16), S(MS_HNEGMU, 16)
        for tg in range(16):
            p.op("s", "activation", S(HJUNK_OFF, 1024, BF16), FINt(tg), AF.Square,
                 accum_out=hsq[:, tg:tg + 1])
            p.op("s", "activation", S(HJUNK_OFF, 1024, BF16), FINt(tg), AF.Copy,
                 accum_out=hsum[:, tg:tg + 1])
        vsc = p.di("s", sSC)
        t0 = S(TMP_A, 512)[:, 0:16]
        t1 = S(TMP_B, 512)[:, 0:16]
        p.wait("v", sSC, vsc)
        p.op("v", "tensor_scalar_mul", hsum, hsum, 1.0 / 1024)
        p.op("v", "drain")
        p.op("v", "tensor_scalar_mul", hsq, hsq, 1.0 / 1024)
        p.op("v", "drain")
        p.op("v", "tensor_mul", t0, hsum, hsum)
        p.op("v", "drain")
        p.op("v", "tensor_sub", t1, hsq, t0)
        p.op("v", "tensor_scalar_mul", hneg, hsum, -1.0)
        vve = p.di("v", sVE)
        p.wait("s", sVE, vve)
        p.op("s", "activation", t0, t1, AF.Sqrt, bias=S(MS_EPS, 1))
        vsc = p.di("s", sSC)
        p.wait("v", sSC, vsc)
        p.op("v", "reciprocal", hrstd, t0)
        p.op("v", "drain")
        p.wait("v", sSP, rv)
        p.wait("t", sWL, vg_identbf)
        pe_tr = {}
        ve_out = {}
        for tg in range(16):
            zpar = tg % 2
            zb = S(ZBUF_OFF + zpar * 4096, 1024, F32)
            if tg >= 2:
                p.wait("v", sPE, pe_tr[tg - 2])
            p.op("v", "tensor_scalar", zb, FINt(tg), hneg[:, tg:tg + 1],
                 hrstd[:, tg:tg + 1], ALU.add, ALU.mult)
            ve_z = p.di("v", sVE)
            p.wait("t", sVE, ve_z)
            if tg >= 2:
                p.wait("t", sVE, ve_out[tg - 2])
            for c in range(8):
                trp = P(2 * zpar + c // 4, (c % 4) * 128, 128)
                p.op("t", "transpose", trp, zb[:, c * 128:(c + 1) * 128],
                     S(MS_IDENTF, 128))
            pe_tr[tg] = p.di("t", sPE)
            p.wait("v", sPE, pe_tr[tg])
            for half4 in range(2):
                trp = P(2 * zpar + half4, 0, 512).rearrange(
                    "p (c j) -> p c j", c=4)
                dst = S(R2_OFF, 16384).rearrange(
                    "p (c t) -> p c t", c=8)[:, half4 * 4:(half4 + 1) * 4,
                                             tg * 128:(tg + 1) * 128]
                p.op("v", "tensor_add", dst, dst, trp)
            ve_out[tg] = p.di("v", sVE)
            p.wait("y", sVE, ve_out[tg])
            p.dma("y", sSP,
                  ydram[:, :].rearrange("(c p) t -> p c t", c=8, p=128)[
                      :, :, tg * 128:(tg + 1) * 128],
                  S(R2_OFF, 16384).rearrange("p (c t) -> p c t", c=8)[
                      :, :, tg * 128:(tg + 1) * 128])
        if "hdbg" in tapd:
            p.wait("y", sVE)
            p.dma("y", sSP, tapd["hdbg"][:, 0:16], hsum)
            p.dma("y", sSP, tapd["hdbg"][:, 16:32], hsq)
            p.dma("y", sSP, tapd["hdbg"][:, 32:48], hrstd)
            p.dma("y", sSP, tapd["hdbg"][:, 48:64], hneg)
            p.wait("y", sSP)
        p.wait("y", sSP)
        return p

    spill0 = spill1 = 0
    if upto_i >= 1:
        phase_B(0)
    if upto_i >= 2:
        phase_C1(0)
    if upto_i >= 3:
        phase_C2(0)
    if upto_i >= 4:
        _, spill0 = phase_D(0)
    if upto_i >= 5:
        phase_E(0, spill0)
    if upto_i >= 6:
        phase_B(1)
    if upto_i >= 7:
        phase_C1(1)
    if upto_i >= 8:
        phase_C2(1)
    if upto_i >= 9:
        _, spill1 = phase_D(1)
    if upto_i >= 10:
        phase_E(1, spill1)
    if upto_i >= 11:
        phase_F()
    if upto_i >= 12:
        phase_H()

    for pr in progs:
        pr.replay_into(nc)

    arena_cm.__exit__(None, None, None)
    ps_cm.__exit__(None, None, None)
    return nc


def _prep_inputs(ins):
    f = np.float32
    x = np.asarray(ins["L0_attn_x"], f)
    base = {
        "wk": np.ascontiguousarray(np.asarray(ins["L0_attn_W_K"], f)),
        "wugg": np.ascontiguousarray(np.asarray(ins["L0_attn_W_ugg"], f)),
        "wout": np.ascontiguousarray(np.asarray(ins["L0_attn_W_out"], f)),
        "rms_col": np.ascontiguousarray(
            np.asarray(ins["L0_attn_rms_scale"], f).reshape(8, 128).T),
        "split_col": np.ascontiguousarray(
            np.asarray(ins["L0_attn_split_scale"], f).reshape(8, 128).T),
        "gwt": np.ascontiguousarray(np.asarray(ins["L0_ffn_gate_w"], f).T),
        "gbias": np.ascontiguousarray(
            np.asarray(ins["L0_ffn_gate_bias"], f).reshape(1, NE)),
        "eg": np.ascontiguousarray(
            np.asarray(ins["L0_ffn_expert_gate"], f).transpose(0, 2, 1)),
        "eu": np.ascontiguousarray(
            np.asarray(ins["L0_ffn_expert_up"], f).transpose(0, 2, 1)),
        "ed": np.ascontiguousarray(
            np.asarray(ins["L0_ffn_expert_down"], f).transpose(0, 2, 1)),
        "sg": np.ascontiguousarray(np.asarray(ins["L0_ffn_shared_gate"], f).T),
        "su": np.ascontiguousarray(np.asarray(ins["L0_ffn_shared_up"], f).T),
        "sd": np.ascontiguousarray(np.asarray(ins["L0_ffn_shared_down"], f).T),
        "osc": np.full((128, 128), 1.0 / 1024, f),
        "ident": np.eye(128, dtype=f),
        "ones1": np.ones((1, 128), f),
    }
    return [dict(base, xin=np.ascontiguousarray(x[b].T)) for b in range(NCORE)]


def kernel(**inputs):
    in_maps = _prep_inputs(inputs)
    nc = build_program()
    res = run_bass_kernel_spmd(nc, in_maps, core_ids=list(range(NCORE)))
    y = np.stack([r["y"] for r in res.results])
    return np.ascontiguousarray(y.transpose(0, 2, 1))


# version 08
# speedup vs baseline: 1.1966x; 1.0572x over previous
import numpy as np
import concourse.bass as bass
import concourse.mybir as mybir
from concourse.bass_utils import run_bass_kernel_spmd

F32 = mybir.dt.float32
BF16 = mybir.dt.bfloat16
U8 = mybir.dt.uint8
AF = mybir.ActivationFunctionType
ALU = mybir.AluOpType
AX = mybir.AxisListType

D = 1024
T = 2048
TH = 1024
NE = 32
NCORE = 8

X_OFF = 0
XN_OFF = 32768
X2N_OFF = 49152
HG_OFF = 81920
ATTN_OFF = 114688
SLABA_OFF = 131072
WOUT_OFF = 147456
BC_OFF = 180224
TMP_OFF = 188416
MISC_OFF = 201728
ARENA_BYTES = 209408
APRIME_OFF = 32768
SILU_OFF = 196608
FIN_OFF = 81920
FSLOT_OFF = 147456
R2_OFF = 0
ZBUF_OFF = BC_OFF
HJUNK_OFF = TMP_OFF
TMP_STAGE = TMP_OFF
TMP_A = TMP_OFF + 2304
TMP_B = TMP_OFF + 4352
TMP_C = TMP_OFF + 6400
TMP_D = TMP_OFF + 8448
TMP_E = TMP_OFF + 10496
MS_SQ0 = MISC_OFF
MS_SQ1 = MISC_OFF + 1024
MS_WTM = MISC_OFF + 2048
MS_KLAST = MISC_OFF + 4096
MS_CARRY = MISC_OFF + 4160
MS_SCOL = MISC_OFF + 4224
MS_RMS = MISC_OFF + 4256
MS_SPL = MISC_OFF + 4288
MS_BIASB = MISC_OFF + 4320
MS_IDENTBF = MISC_OFF + 4448
MS_OSCBF = MISC_OFF + 4704
MS_HSUM = MISC_OFF + 4960
MS_HSQ = MISC_OFF + 5024
MS_HRSTD = MISC_OFF + 5088
MS_HNEGMU = MISC_OFF + 5152
MS_OSCF = MISC_OFF + 5248
MS_IDENTF = MISC_OFF + 5760
MS_GWT = MISC_OFF + 6272
MS_ONES1 = MISC_OFF + 6784
MS_EPS = MISC_OFF + 7424
MS_GBIAS = MISC_OFF + 7296
E_SCF = SLABA_OFF
E_SCTM = SLABA_OFF + 4096
E_SC = SLABA_OFF + 5120
E_EQ = SLABA_OFF + 6144
E_SEL = SLABA_OFF + 7168
E_X8 = SLABA_OFF + 8192
E_WRAW = SLABA_OFF + 9216
E_M1 = SLABA_OFF + 10240
E_M2 = SLABA_OFF + 10368
E_GS = SLABA_OFF + 10496
E_G1 = SLABA_OFF + 10624
E_G2 = SLABA_OFF + 10656
E_GM = SLABA_OFF + 10688
E_MI = SLABA_OFF + 10816
E_GI = SLABA_OFF + 10848
E_DEN = SLABA_OFF + 10880
E_REC = SLABA_OFF + 10912
BC2_MU = SLABA_OFF + 11264
BC2_RSTD = SLABA_OFF + 13312

EB = {F32: 4, BF16: 2}

PH_ORDER = ["A", "B0", "C10", "C20", "D0", "E0",
            "B1", "C11", "C21", "D1", "E1", "F", "H"]


class Sem:
    def __init__(self, nc, name):
        self.h = nc.alloc_semaphore(name=name)
        self.v = 0


def _replay(eng, q):
    for it in q:
        k = it[0]
        if k == "w":
            eng.wait_ge(it[1], it[2])
        elif k == "o":
            getattr(eng, it[1])(*it[2], **it[3])
        elif k == "d":
            eng.dma_start(it[1], it[2]).then_inc(it[3], 16)
        else:
            eng.drain().then_inc(it[1], it[2])


class Prog:
    def __init__(self):
        self.q = {k: [] for k in ("t", "v", "s", "g", "y")}
        self.hwm = {k: {} for k in ("t", "v", "s", "g", "y")}

    def wait(self, e, sem, val=None):
        v = sem.v if val is None else val
        if v > 0 and v > self.hwm[e].get(id(sem), -1):
            self.hwm[e][id(sem)] = v
            self.q[e].append(("w", sem.h, v))

    def op(self, e, meth, *args, **kw):
        self.q[e].append(("o", meth, args, kw))

    def dma(self, e, sem, out, in_):
        sem.v += 16
        self.q[e].append(("d", out, in_, sem.h))
        return sem.v

    def di(self, e, sem, n=1):
        sem.v += n
        self.q[e].append(("i", sem.h, n))
        return sem.v

    def replay_into(self, nc):
        with nc.Block() as blk:
            if self.q["t"]:
                @blk.tensor
                def _(eng, q=self.q["t"]):
                    _replay(eng, q)
            if self.q["v"]:
                @blk.vector
                def _(eng, q=self.q["v"]):
                    _replay(eng, q)
            if self.q["s"]:
                @blk.scalar
                def _(eng, q=self.q["s"]):
                    _replay(eng, q)
            if self.q["g"]:
                @blk.gpsimd
                def _(eng, q=self.q["g"]):
                    _replay(eng, q)
            if self.q["y"]:
                @blk.sync
                def _(eng, q=self.q["y"]):
                    _replay(eng, q)


def build_program(debug_taps=(), upto="H"):
    upto_i = PH_ORDER.index(upto)
    nc = bass.Bass("TRN2", target_bir_lowering=False)

    xin = nc.dram_tensor("xin", [D, T], F32, kind="ExternalInput")
    wk = nc.dram_tensor("wk", [D, 2048], F32, kind="ExternalInput")
    wugg = nc.dram_tensor("wugg", [D, 6144], F32, kind="ExternalInput")
    wout = nc.dram_tensor("wout", [2048, D], F32, kind="ExternalInput")
    rms_col = nc.dram_tensor("rms_col", [128, 8], F32, kind="ExternalInput")
    split_col = nc.dram_tensor("split_col", [128, 8], F32, kind="ExternalInput")
    gwt = nc.dram_tensor("gwt", [D, NE], F32, kind="ExternalInput")
    gbias = nc.dram_tensor("gbias", [1, NE], F32, kind="ExternalInput")
    eg = nc.dram_tensor("eg", [NE, D, 512], F32, kind="ExternalInput")
    eu = nc.dram_tensor("eu", [NE, D, 512], F32, kind="ExternalInput")
    ed = nc.dram_tensor("ed", [NE, 512, D], F32, kind="ExternalInput")
    sg = nc.dram_tensor("sg", [D, D], F32, kind="ExternalInput")
    su = nc.dram_tensor("su", [D, D], F32, kind="ExternalInput")
    sd = nc.dram_tensor("sd", [D, D], F32, kind="ExternalInput")
    osc = nc.dram_tensor("osc", [128, 128], F32, kind="ExternalInput")
    ident = nc.dram_tensor("ident", [128, 128], F32, kind="ExternalInput")
    ones1 = nc.dram_tensor("ones1", [1, 128], F32, kind="ExternalInput")
    ydram = nc.dram_tensor("y", [D, T], F32, kind="ExternalOutput")
    r2d = nc.dram_tensor("r2d", [D, T], F32,
                         kind="ExternalOutput" if "r2" in debug_taps else "Internal")
    tapd = {}
    for name, shape, dt in (("xn", [D, TH], BF16), ("hg", [2048, TH], BF16),
                            ("attn", [D, TH], BF16), ("x2n", [D, T], BF16),
                            ("wtm", [128, 512], F32), ("fin", [128, 16384], F32),
                            ("scores", [32, TH], F32), ("rdbg", [128, 1104], F32),
                            ("hdbg", [128, 64], F32)):
        if name in debug_taps:
            tapd[name] = nc.dram_tensor("tap_" + name, shape, dt, kind="ExternalOutput")

    arena_cm = nc.sbuf_tensor("arena", [128, ARENA_BYTES], U8)
    ps_cm = nc.psum_tensor("ps", [128, 4096], F32)
    arena = arena_cm.__enter__()
    psA = ps_cm.__enter__()

    def S(off, cols, dt=F32, parts=128, p0=0):
        return arena[p0:p0 + parts, off:off + cols * EB[dt]].bitcast(dt)

    def P(bank, off=0, cols=512, parts=128):
        return psA[0:parts, bank * 512 + off: bank * 512 + off + cols]

    sLD = Sem(nc, "sLD")
    sWL = Sem(nc, "sWL")
    sSP = Sem(nc, "sSP")
    sPE = Sem(nc, "sPE")
    sSC = Sem(nc, "sSC")
    sVE = Sem(nc, "sVE")

    progs = []

    def phase():
        p = Prog()
        progs.append(p)
        return p

    def Xc(c, tc, n=512):
        return S(X_OFF + c * 4096, 1024)[:, tc:tc + n]

    def XNc(kc, tc, n=512):
        return S(XN_OFF + kc * 2048, 1024, BF16)[:, tc:tc + n]

    def X2Nc(kc, col, n=512):
        return S(X2N_OFF + kc * 4096, 2048, BF16)[:, col:col + n]

    def HGc(hc, tc, n=512):
        return S(HG_OFF + hc * 2048, 1024, BF16)[:, tc:tc + n]

    def ATTNc(c, tc, n=512):
        return S(ATTN_OFF + c * 2048, 1024, BF16)[:, tc:tc + n]

    def FINt(tg):
        return S(FIN_OFF + tg * 4096, 1024)

    ones1_sb = S(MS_ONES1, 128, F32, parts=1)
    gbias_sb = S(MS_GBIAS, 32, F32, parts=1)
    xload_val = [0]
    r2_reload_val = [0]
    wl_slot = {}

    def emit_c1_slot_load(p, hc, wait_pe_val=None):
        slot_off = SLABA_OFF + (hc % 2) * 8192
        if wait_pe_val:
            p.wait("g", sPE, wait_pe_val)
        v = 0
        for wi, (src, col0) in enumerate(
                ((wk, hc * 128), (wugg, hc * 128),
                 (wugg, 2048 + hc * 128), (wugg, 4096 + hc * 128))):
            v = p.dma("g", sWL,
                      S(slot_off + wi * 2048, 1024, BF16).rearrange(
                          "p (kc j) -> p kc j", kc=8),
                      src[:, col0:col0 + 128].rearrange(
                          "(kc p) j -> p kc j", kc=8, p=128))
        wl_slot[hc] = v

    pA = phase()
    va_x = pA.dma("y", sLD, S(X_OFF, 8192).rearrange("p (c t) -> p c t", c=8),
                  xin[:, :].rearrange("(c p) t -> p c t", c=8, p=128)[:, :, 0:TH])
    pA.dma("y", sLD, S(MS_RMS, 8), rms_col[:, :])
    pA.dma("y", sLD, S(MS_SPL, 8), split_col[:, :])
    va_oscf = pA.dma("y", sLD, S(MS_OSCF, 128), osc[:, :])
    va_identf = pA.dma("y", sLD, S(MS_IDENTF, 128), ident[:, :])
    pA.dma("y", sLD, ones1_sb, ones1[:, :])
    va_small = pA.dma("y", sLD, gbias_sb, gbias[:, :])
    vg_identbf = pA.dma("g", sWL, S(MS_IDENTBF, 128, BF16), ident[:, :])
    vg_oscbf = pA.dma("g", sWL, S(MS_OSCBF, 128, BF16), osc[:, :])
    vg_gwt = pA.dma("g", sWL,
                    S(MS_GWT, 256, BF16).rearrange("p (kc e) -> p kc e", kc=8),
                    gwt[:, :].rearrange("(kc p) e -> p kc e", kc=8, p=128))
    vg_wout = pA.dma("g", sWL,
                     S(WOUT_OFF, 16384, BF16).rearrange("p (kc o) -> p kc o", kc=16),
                     wout[:, :].rearrange("(kc p) o -> p kc o", kc=16, p=128))
    emit_c1_slot_load(pA, 0)
    emit_c1_slot_load(pA, 1)
    pA.op("v", "memset", S(MS_KLAST, 16), 0.0)
    pA.op("v", "memset", S(MS_EPS, 1), 1e-5)
    pA.op("v", "memset", S(MS_CARRY, 16), 0.0)
    pA.wait("v", sLD, va_small)
    pA.op("v", "tensor_mul", S(MS_SCOL, 8), S(MS_RMS, 8), S(MS_SPL, 8))
    pA.di("v", sVE)

    def phase_B(h):
        p = phase()
        xv = va_x if h == 0 else xload_val[0]
        p.wait("t", sLD, xv)
        p.wait("t", sLD, va_oscf)
        p.wait("t", sWL, vg_oscbf)
        p.wait("s", sLD, xv)
        scol = S(MS_SCOL, 8)
        vpe_prev = 0
        ve_end_prev = 0
        for tt in range(2):
            tc = tt * 512
            mu_ps = P(4 * tt + 0)
            sq_ps = P(4 * tt + 1)
            sqv = []
            for c in range(8):
                if c == 0 and vpe_prev:
                    p.wait("s", sPE, vpe_prev)
                    p.wait("s", sVE, ve_end_prev)
                p.op("s", "activation", S(HG_OFF + c * 1024, 512, BF16),
                     Xc(c, tc), AF.Square)
                sqv.append(p.di("s", sSC))
            for c in range(8):
                p.op("t", "matmul", mu_ps, S(MS_OSCF, 128), Xc(c, tc),
                     start=(c == 0), stop=(c == 7))
            for c in range(8):
                p.wait("t", sSC, sqv[c])
                p.op("t", "matmul", sq_ps, S(MS_OSCBF, 128, BF16),
                     S(HG_OFF + c * 1024, 512, BF16),
                     start=(c == 0), stop=(c == 7))
            vpe = p.di("t", sPE)
            vpe_prev = vpe
            t0, t1, t2, t3, t4, t5 = (S(TMP_A, 512), S(TMP_B, 512), S(TMP_C, 512),
                                      S(TMP_D, 512), S(TMP_E, 512),
                                      S(TMP_STAGE, 512))
            p.wait("s", sPE, vpe)
            p.op("s", "activation", t0, mu_ps, AF.Square)
            p.op("s", "copy", S(BC_OFF, 1024)[:, tc:tc + 512], mu_ps)
            vsq = p.di("s", sSC)
            p.wait("v", sPE, vpe)
            p.wait("v", sSC, vsq)
            p.op("v", "tensor_sub", t1, sq_ps, t0)
            vve = p.di("v", sVE)
            p.wait("s", sVE, vve)
            p.op("s", "activation", t2, t1, AF.Sqrt, bias=S(MS_EPS, 1))
            p.op("s", "activation", t3, t1, AF.Sqrt)
            vsc = p.di("s", sSC)
            p.wait("v", sSC, vsc)
            p.op("v", "reciprocal", S(BC_OFF + 4096, 1024)[:, tc:tc + 512], t2)
            p.op("v", "scalar_tensor_tensor", t0, t2, 1e-8, t3, ALU.mult, ALU.add)
            p.op("v", "reciprocal", t4, t0)
            for c in range(8):
                p.op("v", "tensor_sub", t5, Xc(c, tc), mu_ps)
                p.op("v", "scalar_tensor_tensor", XNc(c, tc), t5,
                     scol[:, c:c + 1], t4, ALU.mult, ALU.mult)
            ve_end_prev = p.di("v", sVE)
        if "xn" in tapd and h == 0:
            p.wait("y", sVE)
            for c in range(8):
                p.dma("y", sSP, tapd["xn"][c * 128:(c + 1) * 128, :],
                      S(XN_OFF + c * 2048, 1024, BF16))
            p.wait("y", sSP)
        return p

    def phase_C1(h):
        p = phase()
        if h == 1:
            emit_c1_slot_load(p, 0)
            emit_c1_slot_load(p, 1)
        sc_done = {}
        ve_done = {}
        pe_done = {}
        for hc in range(16):
            slot_off = SLABA_OFF + (hc % 2) * 8192
            for tt in range(2):
                k = hc * 2 + tt
                par = k % 2
                pk, pgi, pu, pgo = (P(4 * par + 0), P(4 * par + 1),
                                    P(4 * par + 2), P(4 * par + 3))
                p.wait("t", sWL, wl_slot[hc])
                if k >= 2:
                    p.wait("t", sSC, sc_done[k - 2])
                    p.wait("t", sVE, ve_done[k - 2])
                for wi, dst in ((0, pk), (2, pgi), (1, pu), (3, pgo)):
                    for kc in range(8):
                        p.op("t", "matmul", dst,
                             S(slot_off + wi * 2048 + kc * 256, 128, BF16),
                             XNc(kc, tt * 512), start=(kc == 0), stop=(kc == 7))
                pe_done[k] = p.di("t", sPE)
                stage = S(TMP_STAGE, 514)
                tmKm, tmGi, tmGo, tmU, tmH = (S(TMP_A, 512), S(TMP_B, 512),
                                              S(TMP_C, 512), S(TMP_D, 512),
                                              S(TMP_E, 512))
                p.wait("s", sPE, pe_done[k])
                if k >= 1:
                    p.wait("s", sVE, ve_done[k - 1])
                p.op("s", "copy", stage[:, 0:1], S(MS_KLAST, 16)[:, hc:hc + 1])
                p.op("s", "activation", stage[:, 1:513], pk, AF.Sigmoid)
                p.op("s", "drain")
                p.op("s", "copy", S(MS_KLAST, 16)[:, hc:hc + 1], stage[:, 512:513])
                p.op("s", "activation", tmKm, pk, AF.Sigmoid, scale=-1.0)
                p.op("s", "activation", tmGi, pgi, AF.Sigmoid)
                p.op("s", "activation", tmGo, pgo, AF.Sigmoid)
                sc_done[k] = p.di("s", sSC)
                p.wait("v", sSC, sc_done[k])
                p.op("v", "tensor_mul", tmGi, tmGi, tmKm)
                p.op("v", "tensor_mul", tmU, pu, tmGi)
                p.op("v", "tensor_tensor_scan", tmH, stage[:, 0:512], tmU,
                     S(MS_CARRY, 16)[:, hc:hc + 1], ALU.mult, ALU.add)
                p.op("v", "drain")
                p.op("v", "tensor_copy", S(MS_CARRY, 16)[:, hc:hc + 1],
                     tmH[:, 511:512])
                p.op("v", "tensor_mul", HGc(hc, tt * 512), tmH, tmGo)
                ve_done[k] = p.di("v", sVE)
            if hc + 2 < 16:
                emit_c1_slot_load(p, hc + 2, wait_pe_val=pe_done[hc * 2 + 1])
        if "hg" in tapd and h == 0:
            p.wait("y", sVE)
            for hc in range(16):
                p.dma("y", sSP, tapd["hg"][hc * 128:(hc + 1) * 128, :],
                      S(HG_OFF + hc * 2048, 1024, BF16))
            p.wait("y", sSP)
        return p

    def phase_C2(h):
        p = phase()
        p.wait("t", sWL, vg_wout)
        ve2 = {}
        for tt in range(2):
            tc = tt * 512
            for seth in range(2):
                m = tt * 2 + seth
                par = m % 2
                if m >= 2:
                    p.wait("t", sVE, ve2[m - 2])
                for oc in range(4):
                    c = seth * 4 + oc
                    dst = P(4 * par + oc)
                    for kc in range(16):
                        p.op("t", "matmul", dst,
                             S(WOUT_OFF + kc * 2048 + c * 256, 128, BF16),
                             HGc(kc, tc), start=(kc == 0), stop=(kc == 15))
                vpe = p.di("t", sPE)
                p.wait("v", sPE, vpe)
                for oc in range(4):
                    c = seth * 4 + oc
                    tmp0 = S(TMP_A, 512)
                    p.op("v", "tensor_sub", tmp0, Xc(c, tc),
                         S(BC_OFF, 1024)[:, tc:tc + 512])
                    p.op("v", "tensor_mul", tmp0, tmp0,
                         S(BC_OFF + 4096, 1024)[:, tc:tc + 512])
                    p.op("v", "tensor_add", ATTNc(c, tc), tmp0, P(4 * par + oc))
                ve2[m] = p.di("v", sVE)
        if "attn" in tapd and h == 0:
            p.wait("y", sVE)
            for c in range(8):
                p.dma("y", sSP, tapd["attn"][c * 128:(c + 1) * 128, :],
                      S(ATTN_OFF + c * 2048, 1024, BF16))
            p.wait("y", sSP)
        return p

    def phase_D(h):
        p = phase()
        p.wait("t", sLD, va_oscf)
        vpe3_prev = 0
        ve_end_prev = 0
        for tt in range(2):
            tc = tt * 512
            mu_ps = P(4 * tt + 0)
            sq_ps = P(4 * tt + 1)
            sqv = []
            if tt == 0:
                t0, t1, t2 = S(TMP_A, 512), S(TMP_B, 512), S(TMP_C, 512)
                bc2m, bc2r = S(BC2_MU, 512), S(BC2_RSTD, 512)
            else:
                t0, t1, t2 = (S(TMP_E, 512), S(TMP_STAGE, 512),
                              S(SLABA_OFF + 5120, 512))
                bc2m, bc2r = S(SLABA_OFF + 7168, 512), S(SLABA_OFF + 9216, 512)
            for c in range(8):
                if c == 0 and vpe3_prev:
                    p.wait("s", sPE, vpe3_prev)
                p.op("s", "activation", S(HG_OFF + c * 1024, 512, BF16),
                     ATTNc(c, tc), AF.Square)
                sqv.append(p.di("s", sSC))
            for c in range(8):
                p.op("t", "matmul", mu_ps, S(MS_OSCBF, 128, BF16), ATTNc(c, tc),
                     start=(c == 0), stop=(c == 7))
            for c in range(8):
                p.wait("t", sSC, sqv[c])
                p.op("t", "matmul", sq_ps, S(MS_OSCBF, 128, BF16),
                     S(HG_OFF + c * 1024, 512, BF16),
                     start=(c == 0), stop=(c == 7))
            vpe = p.di("t", sPE)
            p.wait("s", sPE, vpe)
            p.op("s", "activation", t0, mu_ps, AF.Square)
            p.op("s", "copy", bc2m, mu_ps)
            vsq = p.di("s", sSC)
            p.wait("v", sPE, vpe)
            p.wait("v", sSC, vsq)
            p.op("v", "tensor_sub", t1, sq_ps, t0)
            vve = p.di("v", sVE)
            p.wait("s", sVE, vve)
            p.op("s", "activation", t2, t1, AF.Sqrt, bias=S(MS_EPS, 1))
            vsc = p.di("s", sSC)
            p.wait("v", sSC, vsc)
            p.op("v", "reciprocal", bc2r, t2)
            for c in range(8):
                tmp0 = S(TMP_D, 512)
                p.op("v", "tensor_sub", tmp0, ATTNc(c, tc), bc2m)
                p.op("v", "tensor_mul", tmp0, tmp0, bc2r)
                p.op("v", "tensor_add", Xc(c, tc), tmp0, Xc(c, tc))
            vd2 = p.di("v", sVE)
            mu3_ps = P(4 * tt + 2)
            sq3_ps = P(4 * tt + 3)
            sq3 = []
            p.wait("s", sVE, vd2)
            for c in range(8):
                p.op("s", "activation", S(HG_OFF + 8192 + c * 1024, 512, BF16),
                     Xc(c, tc), AF.Square)
                sq3.append(p.di("s", sSC))
            p.wait("t", sVE, vd2)
            for c in range(8):
                p.op("t", "matmul", mu3_ps, S(MS_OSCF, 128), Xc(c, tc),
                     start=(c == 0), stop=(c == 7))
            for c in range(8):
                p.wait("t", sSC, sq3[c])
                p.op("t", "matmul", sq3_ps, S(MS_OSCBF, 128, BF16),
                     S(HG_OFF + 8192 + c * 1024, 512, BF16),
                     start=(c == 0), stop=(c == 7))
            vpe3 = p.di("t", sPE)
            vpe3_prev = vpe3
            p.wait("s", sPE, vpe3)
            p.op("s", "activation", t0, mu3_ps, AF.Square)
            vsq3 = p.di("s", sSC)
            p.wait("v", sPE, vpe3)
            p.wait("v", sSC, vsq3)
            p.op("v", "tensor_sub", t1, sq3_ps, t0)
            vve3 = p.di("v", sVE)
            p.wait("s", sVE, vve3)
            p.op("s", "activation", t2, t1, AF.Sqrt, bias=S(MS_EPS, 1))
            vsc3 = p.di("s", sSC)
            p.wait("v", sSC, vsc3)
            p.op("v", "reciprocal", t0, t2)
            for c in range(8):
                tmp0 = S(TMP_D, 512)
                p.op("v", "tensor_sub", tmp0, Xc(c, tc), mu3_ps)
                p.op("v", "tensor_mul", X2Nc(c, h * 1024 + tc), tmp0, t0)
            ve_end_prev = p.di("v", sVE)
        p.wait("y", sVE)
        spill_v = p.dma(
            "y", sSP,
            r2d[:, :].rearrange("(c p) t -> p c t", c=8, p=128)[:, :, h * TH:(h + 1) * TH],
            S(X_OFF, 8192).rearrange("p (c t) -> p c t", c=8))
        return p, spill_v

    def phase_E(h, spill_v):
        p = phase()
        scf = S(E_SCF, 1024, F32, parts=32)
        p.wait("t", sWL, vg_gwt)
        vs_prev = 0
        for tt in range(2):
            scps = P(0, 0, 512, parts=32)
            if vs_prev:
                p.wait("t", sSC, vs_prev)
            for kc in range(8):
                p.op("t", "matmul", scps, S(MS_GWT + kc * 64, 32, BF16),
                     X2Nc(kc, h * 1024 + tt * 512), start=(kc == 0), stop=(kc == 7))
            vpe = p.di("t", sPE)
            p.wait("s", sPE, vpe)
            p.op("s", "activation", scf[:, tt * 512:(tt + 1) * 512], scps, AF.Sigmoid)
            vs_prev = p.di("s", sSC)
        if "scores" in tapd and h == 0:
            p.wait("y", sSC)
            p.dma("y", sSP, tapd["scores"][:, :], scf)
            p.wait("y", sSP)
        p.wait("t", sLD, va_identf)
        sctm = S(E_SCTM, 256)
        vtr = {}
        for tk in range(8):
            trps = P(2 + tk % 2, 0, 32)
            if tk >= 2:
                p.wait("t", sSC, vtr[tk - 2])
            p.op("t", "transpose", trps, scf[:, tk * 128:(tk + 1) * 128],
                 S(MS_IDENTF, 128)[0:32, 0:32])
            vpe = p.di("t", sPE)
            p.wait("s", sPE, vpe)
            p.op("s", "copy", sctm[:, tk * 32:(tk + 1) * 32], trps)
            vtr[tk] = p.di("s", sSC)
        if h == 0:
            p.wait("t", sLD, va_small)
            bb = P(4, 0, 32)
            p.op("t", "matmul", bb, ones1_sb, gbias_sb, start=True, stop=True)
            vpe = p.di("t", sPE)
            p.wait("s", sPE, vpe)
            p.op("s", "copy", S(MS_BIASB, 32), bb)
            p.di("s", sSC)
        p.wait("v", sSC)
        sc, eqb, sel, x8, wraw = (S(E_SC, 256), S(E_EQ, 256), S(E_SEL, 256),
                                  S(E_X8, 256), S(E_WRAW, 256))
        m1, m2, gs = S(E_M1, 32), S(E_M2, 32), S(E_GS, 32)
        g1, g2 = S(E_G1, 8), S(E_G2, 8)
        gm, mi, gi = S(E_GM, 32), S(E_MI, 8), S(E_GI, 8)
        den, rec = S(E_DEN, 8), S(E_REC, 8)

        def v4(ap):
            return ap.rearrange("p (t g e) -> p t g e", t=8, g=4, e=8)

        def v3g(ap):
            return ap.rearrange("p (t g) -> p t g", t=8, g=4)

        def v3e(ap):
            return ap.rearrange("p (t e) -> p t e", t=8, e=32)

        def col8(ap):
            return ap.rearrange("p (t o) -> p t o", t=8)

        bias_view = S(MS_BIASB, 32).rearrange("p (o e) -> p o e", o=1) \
            .broadcast_to([128, 8, 32])

        def vop(meth, *args, **kw):
            p.op("v", meth, *args, **kw)
            p.op("v", "drain")

        vop("tensor_tensor", v3e(sc), v3e(sctm), bias_view, ALU.add)
        vop("reduce_max", v3g(m1), v4(sc), AX.X)
        m1b = v3g(m1).unsqueeze(3).broadcast_to([128, 8, 4, 8])
        vop("tensor_tensor", v4(eqb), v4(sc), m1b, ALU.is_equal)
        vop("scalar_tensor_tensor", x8, eqb, -1e30, sc, ALU.mult, ALU.add)
        vop("reduce_max", v3g(m2), v4(x8), AX.X)
        vop("tensor_add", gs, m1, m2)
        vop("reduce_max", col8(g1), v3g(gs), AX.X)
        g1b = col8(g1).broadcast_to([128, 8, 4])
        vop("tensor_tensor", v3g(m1), v3g(gs), g1b, ALU.is_equal)
        vop("scalar_tensor_tensor", m2, m1, -1e30, gs, ALU.mult, ALU.add)
        vop("reduce_max", col8(g2), v3g(m2), AX.X)
        g2b = col8(g2).broadcast_to([128, 8, 4])
        vop("tensor_tensor", v3g(gm), v3g(gs), g2b, ALU.is_ge)
        gmb = v3g(gm).unsqueeze(3).broadcast_to([128, 8, 4, 8])
        vop("tensor_tensor", v4(x8), v4(sc), gmb, ALU.mult)
        vop("memset", sel, 0.0)
        for _ in range(8):
            vop("reduce_max", col8(mi), v3e(x8), AX.X)
            vop("tensor_single_scalar", gi, mi, 0.0, ALU.is_gt)
            mib = col8(mi).broadcast_to([128, 8, 32])
            vop("tensor_tensor", v3e(eqb), v3e(x8), mib, ALU.is_equal)
            gib = col8(gi).broadcast_to([128, 8, 32])
            vop("tensor_tensor", v3e(eqb), v3e(eqb), gib, ALU.mult)
            vop("tensor_add", sel, sel, eqb)
            vop("scalar_tensor_tensor", x8, eqb, -1e30, x8, ALU.mult, ALU.add)
        vop("tensor_mul", wraw, sctm, sel)
        vop("reduce_sum", col8(den), v3e(wraw), AX.X)
        vop("tensor_scalar_add", den, den, 1e-20)
        vop("reciprocal", rec, den)
        recb = col8(rec).broadcast_to([128, 8, 32])
        vop("scalar_tensor_tensor",
             v3e(S(MS_WTM, 512)[:, h * 256:h * 256 + 256]),
             v3e(wraw), 2.5, recb, ALU.mult, ALU.mult)
        p.di("v", sVE)
        if h == 0:
            p.wait("y", sSP, spill_v)
            xload_val[0] = p.dma(
                "y", sLD, S(X_OFF, 8192).rearrange("p (c t) -> p c t", c=8),
                xin[:, :].rearrange("(c p) t -> p c t", c=8, p=128)[:, :, TH:])
        if "rdbg" in tapd:
            p.wait("y", sVE)
            p.dma("y", sSP, tapd["rdbg"][0:128, 0:256], sctm)
            p.dma("y", sSP, tapd["rdbg"][0:128, 256:512], sc)
            p.dma("y", sSP, tapd["rdbg"][0:128, 512:768], sel)
            p.dma("y", sSP, tapd["rdbg"][0:128, 768:1024], wraw)
            p.dma("y", sSP, tapd["rdbg"][0:128, 1024:1056], gm)
            p.dma("y", sSP, tapd["rdbg"][0:128, 1056:1088], gs)
            p.dma("y", sSP, tapd["rdbg"][0:128, 1088:1096], den)
            p.dma("y", sSP, tapd["rdbg"][0:128, 1096:1104], g2)
            p.wait("y", sSP)
        if h == 1 and upto_i >= 11:
            emit_exp_load(p, slot_v_global, 0)
            emit_exp_load(p, slot_v_global, 1)
        if "wtm" in tapd and h == 1:
            p.wait("y", sVE)
            p.dma("y", sSP, tapd["wtm"][:, :], S(MS_WTM, 512))
            p.wait("y", sSP)
        if "x2n" in tapd and (h == 1 or upto_i == 5):
            p.wait("y", sVE)
            for c in range(8):
                p.dma("y", sSP, tapd["x2n"][c * 128:(c + 1) * 128, :],
                      S(X2N_OFF + c * 4096, 2048, BF16))
            p.wait("y", sSP)
        return p

    slot_v_global = {}

    def emit_exp_load(p, slot_v, x, wait_pe=None):
        slot_off = FSLOT_OFF + (x % 2) * 24576
        if wait_pe:
            p.wait("g", sPE, wait_pe)
        if x < 2:
            srcs = ((sg[:, x * 512:(x + 1) * 512], 8),
                    (su[:, x * 512:(x + 1) * 512], 8),
                    (sd[x * 512:(x + 1) * 512, :], 4))
        else:
            e = x - 2
            srcs = ((eg[e], 8), (eu[e], 8), (ed[e], 4))
        v = 0
        for wi, (src, kcn) in enumerate(srcs):
            v = p.dma("g", sWL,
                      S(slot_off + wi * 8192, 4096, BF16).rearrange(
                          "p (kc r) -> p kc r", kc=kcn),
                      src.rearrange("(kc p) r -> p kc r", kc=kcn, p=128))
        slot_v[x] = v

    def phase_F():
        p = phase()
        r2_reload_val[0] = p.dma(
            "y", sSP, S(R2_OFF, 8192).rearrange("p (c t) -> p c t", c=4),
            r2d[:, :].rearrange("(c p) t -> p c t", c=8, p=128)[:, 0:4, :])
        slot_v = {}
        if 0 in slot_v_global:
            slot_v.update(slot_v_global)
        else:
            emit_exp_load(p, slot_v, 0)
            emit_exp_load(p, slot_v, 1)
        peG = {}
        scG = {}
        veA = {}
        peD = {}
        veDadd = {}
        peDdone = {}
        for x in range(34):
            slot_off = FSLOT_OFF + (x % 2) * 24576
            p.wait("t", sWL, slot_v[x])
            for tt in range(4):
                lin = x * 4 + tt
                ab = S(APRIME_OFF + (lin % 2) * 4096, 2048, BF16)
                for ic in range(4):
                    L = lin * 4 + ic
                    if L >= 2:
                        p.wait("t", sSC, scG[L - 2])
                    gg = P(ic % 2)
                    gu = P(2 + ic % 2)
                    for kc in range(8):
                        p.op("t", "matmul", gg,
                             S(slot_off + kc * 1024 + ic * 256, 128, BF16),
                             X2Nc(kc, tt * 512), start=(kc == 0), stop=(kc == 7))
                    for kc in range(8):
                        p.op("t", "matmul", gu,
                             S(slot_off + 8192 + kc * 1024 + ic * 256, 128, BF16),
                             X2Nc(kc, tt * 512), start=(kc == 0), stop=(kc == 7))
                    peG[L] = p.di("t", sPE)
                    p.wait("s", sPE, peG[L])
                    if L >= 2:
                        p.wait("s", sVE, veA[L - 2])
                    p.op("s", "activation", S(SILU_OFF + (ic % 2) * 2048, 512, BF16),
                         gg, AF.Silu)
                    p.op("s", "copy",
                         S(SILU_OFF + 1024 + (ic % 2) * 2048, 512, BF16), gu)
                    scG[L] = p.di("s", sSC)
                    p.wait("v", sSC, scG[L])
                    if ic == 0 and lin >= 2:
                        p.wait("v", sPE, peDdone[lin - 2])
                    p.op("v", "tensor_mul", ab[:, ic * 512:(ic + 1) * 512],
                         S(SILU_OFF + (ic % 2) * 2048, 512, BF16),
                         S(SILU_OFF + 1024 + (ic % 2) * 2048, 512, BF16))
                    veA[L] = p.di("v", sVE)
                for ttok in range(4):
                    M = lin * 4 + ttok
                    dpar = ttok % 2
                    dps = psA[:, (4 + 2 * dpar) * 512:(4 + 2 * dpar) * 512 + 1024]
                    if M >= 2:
                        p.wait("t", sVE, veDadd[M - 2])
                    for ic in range(4):
                        p.wait("t", sVE, veA[lin * 4 + ic])
                        for hh in range(2):
                            p.op("t", "matmul", dps[:, hh * 512:(hh + 1) * 512],
                                 ab[:, ic * 512 + ttok * 128:
                                    ic * 512 + ttok * 128 + 128],
                                 S(slot_off + 16384 + ic * 2048 + hh * 1024,
                                   512, BF16),
                                 start=(ic == 0), stop=(ic == 3))
                    peD[M] = p.di("t", sPE)
                    tg = tt * 4 + ttok
                    p.wait("v", sPE, peD[M])
                    if x == 0:
                        p.op("v", "tensor_copy", FINt(tg), dps)
                    elif x == 1:
                        p.op("v", "scalar_tensor_tensor", FINt(tg), dps, 1.0,
                             FINt(tg), ALU.mult, ALU.add)
                    else:
                        p.op("v", "scalar_tensor_tensor", FINt(tg), dps,
                             S(MS_WTM, 512)[:, tg * 32 + (x - 2):tg * 32 + (x - 1)],
                             FINt(tg), ALU.mult, ALU.add)
                    veDadd[M] = p.di("v", sVE)
                peDdone[lin] = peD[lin * 4 + 3]
            if x + 2 < 34:
                emit_exp_load(p, slot_v, x + 2, wait_pe=peD[(x * 4 + 3) * 4 + 3])
        if "fin" in tapd:
            p.wait("y", sVE)
            p.dma("y", sSP, tapd["fin"][:, :], S(FIN_OFF, 16384))
            p.wait("y", sSP)
        return p

    def phase_H():
        p = phase()
        rv = p.dma("y", sSP,
                   S(R2_OFF + 32768, 8192).rearrange("p (c t) -> p c t", c=4),
                   r2d[:, :].rearrange("(c p) t -> p c t", c=8, p=128)[:, 4:8, :])
        hsum, hsq = S(MS_HSUM, 16), S(MS_HSQ, 16)
        hrstd, hneg = S(MS_HRSTD, 16), S(MS_HNEGMU, 16)
        for tg in range(16):
            p.op("s", "activation", S(HJUNK_OFF, 1024, BF16), FINt(tg), AF.Square,
                 accum_out=hsq[:, tg:tg + 1])
            p.op("s", "activation", S(HJUNK_OFF, 1024, BF16), FINt(tg), AF.Copy,
                 accum_out=hsum[:, tg:tg + 1])
        vsc = p.di("s", sSC)
        t0 = S(TMP_A, 512)[:, 0:16]
        t1 = S(TMP_B, 512)[:, 0:16]
        p.wait("v", sSC, vsc)
        p.op("v", "tensor_scalar_mul", hsum, hsum, 1.0 / 1024)
        p.op("v", "drain")
        p.op("v", "tensor_scalar_mul", hsq, hsq, 1.0 / 1024)
        p.op("v", "drain")
        p.op("v", "tensor_mul", t0, hsum, hsum)
        p.op("v", "drain")
        p.op("v", "tensor_sub", t1, hsq, t0)
        p.op("v", "tensor_scalar_mul", hneg, hsum, -1.0)
        vve = p.di("v", sVE)
        p.wait("s", sVE, vve)
        p.op("s", "activation", t0, t1, AF.Sqrt, bias=S(MS_EPS, 1))
        vsc = p.di("s", sSC)
        p.wait("v", sSC, vsc)
        p.op("v", "reciprocal", hrstd, t0)
        p.op("v", "drain")
        p.wait("v", sSP, rv)
        p.wait("t", sWL, vg_identbf)
        pe_tr = {}
        ve_out = {}
        for tg in range(16):
            zpar = tg % 2
            zb = S(ZBUF_OFF + zpar * 4096, 1024, F32)
            if tg >= 2:
                p.wait("v", sPE, pe_tr[tg - 2])
            p.op("v", "tensor_scalar", zb, FINt(tg), hneg[:, tg:tg + 1],
                 hrstd[:, tg:tg + 1], ALU.add, ALU.mult)
            ve_z = p.di("v", sVE)
            p.wait("t", sVE, ve_z)
            if tg >= 2:
                p.wait("t", sVE, ve_out[tg - 2])
            for c in range(8):
                trp = P(2 * zpar + c // 4, (c % 4) * 128, 128)
                p.op("t", "transpose", trp, zb[:, c * 128:(c + 1) * 128],
                     S(MS_IDENTF, 128))
            pe_tr[tg] = p.di("t", sPE)
            p.wait("v", sPE, pe_tr[tg])
            for half4 in range(2):
                trp = P(2 * zpar + half4, 0, 512).rearrange(
                    "p (c j) -> p c j", c=4)
                dst = S(R2_OFF, 16384).rearrange(
                    "p (c t) -> p c t", c=8)[:, half4 * 4:(half4 + 1) * 4,
                                             tg * 128:(tg + 1) * 128]
                p.op("v", "tensor_add", dst, dst, trp)
            ve_out[tg] = p.di("v", sVE)
            p.wait("y", sVE, ve_out[tg])
            p.dma("y", sSP,
                  ydram[:, :].rearrange("(c p) t -> p c t", c=8, p=128)[
                      :, :, tg * 128:(tg + 1) * 128],
                  S(R2_OFF, 16384).rearrange("p (c t) -> p c t", c=8)[
                      :, :, tg * 128:(tg + 1) * 128])
        if "hdbg" in tapd:
            p.wait("y", sVE)
            p.dma("y", sSP, tapd["hdbg"][:, 0:16], hsum)
            p.dma("y", sSP, tapd["hdbg"][:, 16:32], hsq)
            p.dma("y", sSP, tapd["hdbg"][:, 32:48], hrstd)
            p.dma("y", sSP, tapd["hdbg"][:, 48:64], hneg)
            p.wait("y", sSP)
        p.wait("y", sSP)
        return p

    spill0 = spill1 = 0
    if upto_i >= 1:
        phase_B(0)
    if upto_i >= 2:
        phase_C1(0)
    if upto_i >= 3:
        phase_C2(0)
    if upto_i >= 4:
        _, spill0 = phase_D(0)
    if upto_i >= 5:
        phase_E(0, spill0)
    if upto_i >= 6:
        phase_B(1)
    if upto_i >= 7:
        phase_C1(1)
    if upto_i >= 8:
        phase_C2(1)
    if upto_i >= 9:
        _, spill1 = phase_D(1)
    if upto_i >= 10:
        phase_E(1, spill1)
    if upto_i >= 11:
        phase_F()
    if upto_i >= 12:
        phase_H()

    for pr in progs:
        pr.replay_into(nc)

    arena_cm.__exit__(None, None, None)
    ps_cm.__exit__(None, None, None)
    return nc


def _prep_inputs(ins):
    f = np.float32
    x = np.asarray(ins["L0_attn_x"], f)
    base = {
        "wk": np.ascontiguousarray(np.asarray(ins["L0_attn_W_K"], f)),
        "wugg": np.ascontiguousarray(np.asarray(ins["L0_attn_W_ugg"], f)),
        "wout": np.ascontiguousarray(np.asarray(ins["L0_attn_W_out"], f)),
        "rms_col": np.ascontiguousarray(
            np.asarray(ins["L0_attn_rms_scale"], f).reshape(8, 128).T),
        "split_col": np.ascontiguousarray(
            np.asarray(ins["L0_attn_split_scale"], f).reshape(8, 128).T),
        "gwt": np.ascontiguousarray(np.asarray(ins["L0_ffn_gate_w"], f).T),
        "gbias": np.ascontiguousarray(
            np.asarray(ins["L0_ffn_gate_bias"], f).reshape(1, NE)),
        "eg": np.ascontiguousarray(
            np.asarray(ins["L0_ffn_expert_gate"], f).transpose(0, 2, 1)),
        "eu": np.ascontiguousarray(
            np.asarray(ins["L0_ffn_expert_up"], f).transpose(0, 2, 1)),
        "ed": np.ascontiguousarray(
            np.asarray(ins["L0_ffn_expert_down"], f).transpose(0, 2, 1)),
        "sg": np.ascontiguousarray(np.asarray(ins["L0_ffn_shared_gate"], f).T),
        "su": np.ascontiguousarray(np.asarray(ins["L0_ffn_shared_up"], f).T),
        "sd": np.ascontiguousarray(np.asarray(ins["L0_ffn_shared_down"], f).T),
        "osc": np.full((128, 128), 1.0 / 1024, f),
        "ident": np.eye(128, dtype=f),
        "ones1": np.ones((1, 128), f),
    }
    return [dict(base, xin=np.ascontiguousarray(x[b].T)) for b in range(NCORE)]


def kernel(**inputs):
    in_maps = _prep_inputs(inputs)
    nc = build_program()
    res = run_bass_kernel_spmd(nc, in_maps, core_ids=list(range(NCORE)))
    y = np.stack([r["y"] for r in res.results])
    return np.ascontiguousarray(y.transpose(0, 2, 1))
